# version 07
# baseline (speedup 1.0000x reference)
import contextlib
import numpy as np
import concourse.bass as bass
import concourse.mybir as mybir
from concourse.bass_utils import run_bass_kernel_spmd

F32 = mybir.dt.float32
BF16 = mybir.dt.bfloat16
AX = mybir.AxisListType.X
AF = mybir.ActivationFunctionType
ALU = mybir.AluOpType

NCORES = 8
T = 2048
NT = 16
NST = 4
D = 4096
C = 32
KH = 8
QH = 32
NSLOT = 1024
ISQ_R = float(1.0 / np.sqrt(48.0))
ISQ_B = 0.125
BIG = 1.0e30
GC0 = 0.7978845608028654
GC1 = GC0 * 0.044715
EW = 1040

ENGS = ("sync", "gpsimd", "vector", "scalar", "tensor")


class Sched:

    def __init__(self, which, eng, sems, cnt, rec):
        self.which = which
        self.e = eng
        self.sems = sems
        self.cnt = cnt
        self.rec = rec

    def do(self, eng, emit, inc=None, inc_n=1, waits=()):
        if self.which == eng:
            if eng in ("vector", "scalar") and inc is not None and self.cnt[inc] > 0:
                self.e.wait_ge(self.sems[inc], self.cnt[inc])
            for (sname, val) in waits:
                if val is not None and val > 0:
                    self.e.wait_ge(self.sems[sname], val)
            inst = emit()
            if inc is not None:
                inst.then_inc(self.sems[inc], inc_n)
        if inc is not None:
            self.cnt[inc] += inc_n

    def mark(self, key, semname):
        self.rec[key] = self.cnt[semname]

    def get(self, key):
        return self.rec[key]

    def barrier(self):
        base = self.cnt["bY"]
        for j, e in enumerate(ENGS):
            if self.which == e:
                if base + j > 0:
                    self.e.wait_ge(self.sems["bY"], base + j)
                self.e.nop().then_inc(self.sems["bY"], 1)
            self.cnt["bY"] += 1
        if self.which is not None and self.which != ENGS[-1]:
            self.e.wait_ge(self.sems["bY"], base + len(ENGS))


def build_program(gw1, gbias, bu2):
    import os as _os
    PH = _os.environ.get("KPHASES", "1234")
    nc = bass.Bass("TRN2", target_bir_lowering=False)
    dt = nc.dram_tensor

    x_h = dt("x", [T, D], F32, kind="ExternalInput").ap()
    out_h = dt("out", [T, D], F32, kind="ExternalOutput").ap()
    wk_h = dt("wk", [128, C * 1024], F32, kind="ExternalInput").ap()
    wv_h = dt("wv", [128, C * 1024], F32, kind="ExternalInput").ap()
    wq_h = dt("wq", [QH, 128, C * 128], F32, kind="ExternalInput").ap()
    wo_h = dt("wo", [8, 128, C * 512], F32, kind="ExternalInput").ap()
    wqf_h = dt("wqf", [128, C * 64], F32, kind="ExternalInput").ap()
    wr_h = dt("wr", [64, 48], F32, kind="ExternalInput").ap()
    ak_h = dt("ak", [64, NSLOT], F32, kind="ExternalInput").ap()
    av_h = dt("av", [128, C * 1024], F32, kind="ExternalInput").ap()
    wvd_h = dt("wvd", [128, C * 256], F32, kind="ExternalInput").ap()
    wvu_h = dt("wvu", [128, 2 * D], F32, kind="ExternalInput").ap()
    wu1_h = dt("wu1", [128, C * 1024], F32, kind="ExternalInput").ap()
    wu2_h = dt("wu2", [128, 8], F32, kind="ExternalInput").ap()
    bqr_h = dt("bqr", [128, QH], F32, kind="ExternalInput").ap()
    bu1r_h = dt("bu1r", [128, 8], F32, kind="ExternalInput").ap()
    bkrep_h = dt("bkrep", [128, 1024], F32, kind="ExternalInput").ap()
    bvrep_h = dt("bvrep", [128, 1024], F32, kind="ExternalInput").ap()
    borep_h = dt("borep", [128, D], F32, kind="ExternalInput").ap()
    lrrep_h = dt("lrrep", [128, NSLOT], F32, kind="ExternalInput").ap()
    ident_h = dt("ident", [128, 128], F32, kind="ExternalInput").ap()
    bmask_h = dt("bmask", [128, 4], F32, kind="ExternalInput").ap()

    xn_sp = dt("xn_sp", [NT, 128, D], BF16, kind="Internal").ap()
    attn_sp = dt("attn_sp", [128, QH * T], BF16, kind="Internal").ap()
    aux_sp = dt("aux_sp", [NT, 128, D], F32, kind="Internal").ap()
    cc_in = dt("cc_in", [4, 128, EW], F32, kind="Internal")
    cc_out = dt("cc_out", [4, 128, EW], F32, kind="Internal")

    es = contextlib.ExitStack()
    sbt = lambda name, shp, dtype: es.enter_context(nc.sbuf_tensor(name, shp, dtype))

    with es:
        rstd_all = sbt("rstd_all", [128, NT], F32)
        nmr_all = sbt("nmr_all", [128, NT], F32)
        lv_all = sbt("lv_all", [128, NT], F32)
        sig_all = sbt("sig_all", [128, NT], F32)
        gate_all = sbt("gate_all", [128, NT], F32)
        kv_bf = sbt("kv_bf", [128, KH * 128], BF16)
        inv_ks = sbt("inv_ks", [128, KH], F32)
        invmean = sbt("invmean", [128, 1], F32)
        ones_f = sbt("ones_f", [128, 1], F32)
        ones_b = sbt("ones_b", [128, 1], BF16)
        small = sbt("small", [128, 16], F32)
        bmask_sb = sbt("bmask_sb", [128, 4], F32)

        sems = {n: es.enter_context(nc.semaphore(n)) for n in
                ("dS", "dG", "tS", "vS", "sS", "bY", "bG", "cc")}

        cnt_g = {n: 0 for n in sems}
        rec = {}

        def run_phase(body):
            start = dict(cnt_g)
            final = {}

            with nc.Block() as block:
                def emit(which, eng):
                    local = dict(start)
                    body(Sched(which, eng, sems, local, rec))
                    final.update(local)

                @block.sync
                def _(e):
                    emit("sync", e)

                @block.gpsimd
                def _(e):
                    emit("gpsimd", e)

                @block.vector
                def _(e):
                    emit("vector", e)

                @block.scalar
                def _(e):
                    emit("scalar", e)

                @block.tensor
                def _(e):
                    emit("tensor", e)

            cnt_g.clear()
            cnt_g.update(final)

        with contextlib.ExitStack() as a1:
            sbl = lambda n, shp, dty: a1.enter_context(nc.sbuf_tensor(n, shp, dty))
            psl = lambda n, shp: a1.enter_context(nc.psum_tensor(n, shp, F32))
            wk_sb = sbl("wk_sb", [128, C * 1024], BF16)
            wv_sb = sbl("wv_sb", [128, C * 1024], BF16)
            x_t = [sbl("x_t0", [128, D], F32)]
            xn_t = sbl("xn_t", [128, D], F32)
            xn_fm = sbl("xn_fm", [128, D], BF16)
            k_bf = sbl("k_bf", [128, 1024], BF16)
            v_bf = sbl("v_bf", [128, 1024], BF16)
            bk_rep = sbl("bk_rep", [128, 1024], F32)
            bv_rep = sbl("bv_rep", [128, 1024], F32)
            ident_f = sbl("ident_f", [128, 128], F32)
            scr0 = sbl("scr0", [128, EW], F32)
            scr1 = sbl("scr1", [128, 1024], F32)
            exch = sbl("exch", [128, EW], F32)
            ps_tr = [psl(f"ps_tr{i}", [128, 512]) for i in range(2)]
            ps_gen = psl("ps_gen", [128, 1024])
            ps_kv = psl("ps_kv", [128, 1024])
            ps_ks = psl("ps_ks", [128, 8])

            def phase_a1(s):
                d = s.do
                d("gpsimd", lambda: nc.gpsimd.dma_start(out=wk_sb[:], in_=wk_h[:]), inc="dG", inc_n=16)
                d("gpsimd", lambda: nc.gpsimd.dma_start(out=wv_sb[:], in_=wv_h[:]), inc="dG", inc_n=16)
                s.mark("wkv", "dG")
                d("sync", lambda: nc.sync.dma_start(out=bk_rep[:], in_=bkrep_h[:]), inc="dS", inc_n=16)
                d("sync", lambda: nc.sync.dma_start(out=bv_rep[:], in_=bvrep_h[:]), inc="dS", inc_n=16)
                d("sync", lambda: nc.sync.dma_start(out=ident_f[:], in_=ident_h[:]), inc="dS", inc_n=16)
                d("sync", lambda: nc.sync.dma_start(out=bmask_sb[:], in_=bmask_h[:]), inc="dS", inc_n=16)
                s.mark("a1c", "dS")
                d("vector", lambda: nc.vector.memset(ones_f[:], 1.0), inc="vS")
                d("vector", lambda: nc.vector.memset(ones_b[:], 1.0), inc="vS")
                d("vector", lambda: nc.vector.memset(exch[:], 0.0), inc="vS")
                s.mark("exch0", "vS")
                for b in range(4):
                    d("gpsimd", lambda b=b: nc.gpsimd.dma_start(
                        out=cc_in.ap()[b], in_=exch[:]),
                        inc="dG", inc_n=16, waits=(("vS", s.get("exch0")),))
                s.mark("ccz", "dG")

                for i in range(NT):
                    xb = x_t[0]
                    sqc = 13 + (i % 2)
                    w = []
                    if i >= 1:
                        w += [("sS", s.get(("xnact", i - 1))),
                              ("vS", s.get(("xsum", i - 1)))]
                    d("sync", lambda i=i, xb=xb: nc.sync.dma_start(
                        out=xb[:], in_=x_h[i * 128:(i + 1) * 128, :]),
                        inc="dS", inc_n=16, waits=tuple(w))
                    s.mark(("xld", i), "dS")

                    d("vector", lambda xb=xb: nc.vector.reduce_sum(
                        small[:, 0:1], xb[:], axis=AX),
                        inc="vS", waits=(("dS", s.get(("xld", i))),))
                    s.mark(("xsum", i), "vS")
                    w = [("dS", s.get(("xld", i)))]
                    if i >= 1:
                        w.append(("tS", s.get(("trdone", i - 1))))
                    if i >= 2:
                        w.append(("vS", s.get(("varred", i - 2))))
                    d("scalar", lambda xb=xb, sqc=sqc: nc.scalar.activation(
                        xn_t[:], xb[:], AF.Square, accum_out=small[:, sqc:sqc + 1]),
                        inc="sS", waits=tuple(w))
                    s.mark(("ssq", i), "sS")
                    d("vector", lambda: nc.vector.tensor_scalar(
                        small[:, 2:3], small[:, 0:1], 1.0 / D, None, op0=ALU.mult), inc="vS")
                    d("vector", lambda: nc.vector.tensor_mul(
                        small[:, 3:4], small[:, 2:3], small[:, 2:3]), inc="vS")
                    d("vector", lambda sqc=sqc: nc.vector.scalar_tensor_tensor(
                        small[:, 4:5], small[:, sqc:sqc + 1], 1.0 / D, small[:, 3:4],
                        op0=ALU.mult, op1=ALU.subtract),
                        inc="vS", waits=(("sS", s.get(("ssq", i))),))
                    s.mark(("varred", i), "vS")
                    d("vector", lambda: nc.vector.tensor_scalar(
                        small[:, 5:6], small[:, 4:5], 1e-5, None, op0=ALU.add), inc="vS")
                    s.mark(("veps", i), "vS")
                    d("scalar", lambda: nc.scalar.activation(
                        small[:, 6:7], small[:, 5:6], AF.Ln),
                        inc="sS", waits=(("vS", s.get(("veps", i))),))
                    d("scalar", lambda i=i: nc.scalar.activation(
                        rstd_all[:, i:i + 1], small[:, 6:7], AF.Exp, scale=-0.5), inc="sS")
                    s.mark(("rstd", i), "sS")
                    d("vector", lambda i=i: nc.vector.scalar_tensor_tensor(
                        nmr_all[:, i:i + 1], small[:, 2:3], -1.0, rstd_all[:, i:i + 1],
                        op0=ALU.mult, op1=ALU.mult),
                        inc="vS", waits=(("sS", s.get(("rstd", i))),))
                    s.mark(("nmr", i), "vS")
                    d("vector", lambda: nc.vector.reciprocal(
                        small[:, 7:8], small[:, 5:6]), inc="vS")
                    d("vector", lambda: nc.vector.tensor_mul(
                        small[:, 8:9], small[:, 4:5], small[:, 7:8]), inc="vS")
                    s.mark(("ratio", i), "vS")
                    d("scalar", lambda i=i: nc.scalar.activation(
                        lv_all[:, i:i + 1], small[:, 8:9], AF.Ln, bias=1.0),
                        inc="sS", waits=(("vS", s.get(("ratio", i))),))
                    d("scalar", lambda i=i, xb=xb: nc.scalar.activation(
                        xn_t[:], xb[:], AF.Identity,
                        bias=nmr_all[:, i:i + 1], scale=rstd_all[:, i:i + 1]),
                        inc="sS", waits=(("vS", s.get(("nmr", i))),))
                    s.mark(("xnact", i), "sS")

                    for g in range(8):
                        gi = i * 8 + g
                        w = [("sS", s.get(("xnact", i)))]
                        if gi >= 2:
                            w.append(("sS", s.get(("tre", gi - 2))))
                        for q in range(4):
                            cq = 4 * g + q
                            d("tensor", lambda g=g, q=q, cq=cq: nc.tensor.transpose(
                                ps_tr[g % 2][:, q * 128:(q + 1) * 128],
                                xn_t[:, cq * 128:(cq + 1) * 128], ident_f[:]),
                                inc=("tS" if q == 3 else None),
                                waits=tuple(w) if q == 0 else ())
                        s.mark(("trw", gi), "tS")
                        w = [("tS", s.get(("trw", gi)))]
                        if g == 0 and i >= 1:
                            w.append(("dS", s.get(("xnspill", i - 1))))
                        d("scalar", lambda g=g: nc.scalar.activation(
                            xn_fm[:, g * 512:(g + 1) * 512], ps_tr[g % 2][:], AF.Copy),
                            inc="sS", waits=tuple(w))
                        s.mark(("tre", gi), "sS")
                    s.mark(("trdone", i), "tS")
                    s.mark(("evdone", i), "sS")

                    d("sync", lambda i=i: nc.sync.dma_start(
                        out=xn_sp[i], in_=xn_fm[:]),
                        inc="dS", inc_n=16, waits=(("sS", s.get(("evdone", i))),))
                    s.mark(("xnspill", i), "dS")

                    w = [("sS", s.get(("evdone", i))), ("dG", s.get("wkv"))]
                    if i >= 1:
                        w.append(("vS", s.get(("pgenfree", i - 1))))
                    for half in range(2):
                        for c in range(C):
                            d("tensor", lambda half=half, c=c: nc.tensor.matmul(
                                ps_gen[:, half * 512:(half + 1) * 512],
                                xn_fm[:, c * 128:(c + 1) * 128],
                                wk_sb[:, c * 1024 + half * 512: c * 1024 + (half + 1) * 512],
                                start=(c == 0), stop=(c == C - 1)),
                                inc=("tS" if (c == C - 1 and half == 1) else None),
                                waits=tuple(w) if (half == 0 and c == 0) else ())
                    s.mark(("kmm", i), "tS")
                    w = [("tS", s.get(("kmm", i)))]
                    if i >= 1:
                        w.append(("tS", s.get(("kvmm", i - 1))))
                    else:
                        w.append(("dS", s.get("a1c")))
                    d("vector", lambda: nc.vector.tensor_add(
                        scr0[:, 0:1024], ps_gen[:], bk_rep[:]),
                        inc="vS", waits=tuple(w))
                    s.mark(("kfree", i), "vS")
                    d("vector", lambda: nc.vector.tensor_scalar(
                        scr1[:], scr0[:, 0:1024], 0.0, None, op0=ALU.min), inc="vS")
                    s.mark(("kxm", i), "vS")
                    d("scalar", lambda: nc.scalar.activation(
                        scr1[:], scr1[:], AF.Exp),
                        inc="sS", waits=(("vS", s.get(("kxm", i))),))
                    s.mark(("kexp", i), "sS")
                    d("vector", lambda: nc.vector.scalar_tensor_tensor(
                        k_bf[:], scr0[:, 0:1024], 1.0, scr1[:],
                        op0=ALU.add, op1=ALU.max),
                        inc="vS", waits=(("sS", s.get(("kexp", i))),))
                    s.mark(("kbf", i), "vS")

                    for half in range(2):
                        for c in range(C):
                            d("tensor", lambda half=half, c=c: nc.tensor.matmul(
                                ps_gen[:, half * 512:(half + 1) * 512],
                                xn_fm[:, c * 128:(c + 1) * 128],
                                wv_sb[:, c * 1024 + half * 512: c * 1024 + (half + 1) * 512],
                                start=(c == 0), stop=(c == C - 1)),
                                inc=("tS" if (c == C - 1 and half == 1) else None),
                                waits=(("vS", s.get(("kfree", i))),)
                                if (half == 0 and c == 0) else ())
                    s.mark(("vmm", i), "tS")
                    w = [("tS", s.get(("vmm", i)))]
                    if i >= 1:
                        w.append(("tS", s.get(("kvmm", i - 1))))
                    d("vector", lambda: nc.vector.tensor_add(
                        v_bf[:], ps_gen[:], bv_rep[:]),
                        inc="vS", waits=tuple(w))
                    s.mark(("pgenfree", i), "vS")
                    s.mark(("vbf", i), "vS")

                    w = [("vS", s.get(("kbf", i))), ("vS", s.get(("vbf", i)))]
                    if i >= 1:
                        w.append(("vS", s.get(("kvacc", i - 1))))
                    for h in range(KH):
                        d("tensor", lambda h=h: nc.tensor.matmul(
                            ps_kv[:, h * 128:(h + 1) * 128],
                            k_bf[:, h * 128:(h + 1) * 128],
                            v_bf[:, h * 128:(h + 1) * 128],
                            start=(h % 4 == 0), stop=(h % 4 == 3)),
                            waits=tuple(w) if h == 0 else ())
                    for h in range(KH):
                        d("tensor", lambda h=h: nc.tensor.matmul(
                            ps_ks[:, h:h + 1],
                            k_bf[:, h * 128:(h + 1) * 128], ones_b[:],
                            start=(h == 0), stop=(h == KH - 1)),
                            inc=("tS" if h == KH - 1 else None))
                    s.mark(("kvmm", i), "tS")
                    w = [("tS", s.get(("kvmm", i)))]
                    if i == 0:
                        w.append(("dG", s.get("ccz")))
                    d("vector", lambda: nc.vector.tensor_add(
                        exch[:, 0:512], exch[:, 0:512], ps_kv[:, 0:512]),
                        inc="vS", waits=tuple(w))
                    d("vector", lambda: nc.vector.tensor_add(
                        exch[:, 512:1024], exch[:, 512:1024], ps_kv[:, 512:1024]),
                        inc="vS")
                    d("vector", lambda: nc.vector.tensor_add(
                        exch[:, 1024:1032], exch[:, 1024:1032], ps_ks[:]),
                        inc="vS")
                    s.mark(("kvacc", i), "vS")

                import os as _os2
                if _os2.environ.get("KNOCC"):
                    s.barrier()
                    if _os2.environ.get("KDBLBAR"):
                        s.barrier()
                        s.barrier()
                    return
                d("vector", lambda: nc.vector.reduce_sum(
                    exch[:, 1032:1033], lv_all[:], axis=AX), inc="vS")
                s.mark("exfill", "vS")
                if _os2.environ.get("KNOF2"):
                    s.barrier()
                    return
                for b in range(4):
                    w = [("vS", s.get("exfill")), ("dS", s.get("a1c"))]
                    if b >= 1:
                        w.append(("dG", s.get(("ccw", b - 1))))
                    d("vector", lambda b=b: nc.vector.tensor_scalar(
                        scr0[:], exch[:], bmask_sb[:, b:b + 1], None, op0=ALU.mult),
                        inc="vS", waits=tuple(w))
                    s.mark(("ccm", b), "vS")
                    d("gpsimd", lambda b=b: nc.gpsimd.dma_start(
                        out=cc_in.ap()[b], in_=scr0[:]),
                        inc="dG", inc_n=16, waits=(("vS", s.get(("ccm", b))),))
                    s.mark(("ccw", b), "dG")
                if not _os2.environ.get("KNOAR"):
                    d("gpsimd", lambda: nc.gpsimd.collective_compute(
                        "AllReduce", ALU.add,
                        replica_groups=[list(range(NCORES))],
                        ins=[cc_in.ap().opt()], outs=[cc_out.ap().opt()]),
                        inc="cc", waits=(("dG", s.get(("ccw", 3))),))
                s.barrier()

            if "1" in PH:
                run_phase(phase_a1)

        with contextlib.ExitStack() as a2o:
            sbo = lambda n, shp, dty: a2o.enter_context(nc.sbuf_tensor(n, shp, dty))
            vp_sb = sbo("vp_sb", [128, 8 * 256], BF16)
            wr_sb = sbo("wr_sb", [64, 48], BF16)
            ak_sb = sbo("ak_sb", [64, NSLOT], BF16)
            rk_sb = sbo("rk_sb", [64, NSLOT], BF16)

            with contextlib.ExitStack() as tmp:
                av_sb = tmp.enter_context(nc.sbuf_tensor("av_sb", [128, C * 1024], BF16))
                wvd_sb = tmp.enter_context(nc.sbuf_tensor("wvd_sb", [128, C * 256], BF16))
                ps_vp = tmp.enter_context(nc.psum_tensor("ps_vp", [128, 256], F32))

                def phase_a2a(s):
                    d = s.do
                    import os as _os5
                    if _os5.environ.get("KA2AEMPTY"):
                        s.barrier()
                        return
                    for dst, src in ((wr_sb, wr_h), (ak_sb, ak_h)):
                        d("gpsimd", lambda dst=dst, src=src: nc.gpsimd.dma_start(
                            out=dst[:], in_=src[:]), inc="dG", inc_n=16)
                    d("gpsimd", lambda: nc.gpsimd.dma_start(out=av_sb[:], in_=av_h[:]),
                      inc="dG", inc_n=16)
                    d("gpsimd", lambda: nc.gpsimd.dma_start(out=wvd_sb[:], in_=wvd_h[:]),
                      inc="dG", inc_n=16)
                    s.mark("avw", "dG")
                    for n in range(8):
                        w = [("dG", s.get("avw"))]
                        if n >= 1:
                            w.append(("vS", s.get(("vpe", n - 1))))
                        for hc in range(C):
                            d("tensor", lambda hc=hc, n=n: nc.tensor.matmul(
                                ps_vp[:],
                                av_sb[:, hc * 1024 + n * 128: hc * 1024 + (n + 1) * 128],
                                wvd_sb[:, hc * 256:(hc + 1) * 256],
                                start=(hc == 0), stop=(hc == C - 1)),
                                inc=("tS" if hc == C - 1 else None),
                                waits=tuple(w) if hc == 0 else ())
                        s.mark(("vpm", n), "tS")
                        d("vector", lambda n=n: nc.vector.tensor_copy(
                            vp_sb[:, n * 256:(n + 1) * 256], ps_vp[:]),
                            inc="vS", waits=(("tS", s.get(("vpm", n))),))
                        s.mark(("vpe", n), "vS")
                    import os as _os4
                    if _os4.environ.get("KNORK"):
                        s.mark("rkdone", "vS")
                        s.barrier()
                        return
                    for piece in range(4):
                        w = [("dG", s.get("avw")), ("vS", s.cnt["vS"])]
                        d("tensor", lambda piece=piece: nc.tensor.matmul(
                            ps_vp[0:48, 0:256], wr_sb[:],
                            ak_sb[:, piece * 256:(piece + 1) * 256],
                            start=True, stop=True),
                            inc="tS", waits=tuple(w))
                        s.mark(("rkm", piece), "tS")
                        d("vector", lambda piece=piece: nc.vector.tensor_copy(
                            rk_sb[0:48, piece * 256:(piece + 1) * 256],
                            ps_vp[0:48, 0:256]),
                            inc="vS", waits=(("tS", s.get(("rkm", piece))),))
                    s.mark("rkdone", "vS")
                    s.barrier()

                if "2" in PH or "a" in PH:
                    run_phase(phase_a2a)

            a2 = a2o
            sbl = lambda n, shp, dty: a2.enter_context(nc.sbuf_tensor(n, shp, dty))
            psl = lambda n, shp: a2.enter_context(nc.psum_tensor(n, shp, F32))
            wu1_sb = sbl("wu1_sb", [128, C * 1024], BF16)
            wvu_sb = sbl("wvu_sb", [128, 2 * D], BF16)
            wqf_sb = sbl("wqf_sb", [128, C * 64], BF16)
            lr_rep = sbl("lr_rep", [128, NSLOT], F32)
            bu1_sb = sbl("bu1_sb", [128, 8], F32)
            ident_f2 = sbl("ident_f2", [128, 128], F32)
            wu2_sb = sbl("wu2_sb", [128, 8], BF16)
            xn_st = sbl("xn_st", [128, 4 * D], BF16)
            zS = sbl("zS", [128, D], F32)
            t2S = sbl("t2S", [128, D], F32)
            gS = sbl("gS", [128, D], BF16)
            sc_tm = sbl("sc_tm", [128, NSLOT], F32)
            zz = sbl("zz", [128, NSLOT], F32)
            pen = sbl("pen", [128, NSLOT], F32)
            w_t = sbl("w_t", [128, NSLOT], F32)
            qf_b = sbl("qf_b", [64, 128], BF16)
            wT_b = sbl("wT_b", [128, NSLOT], BF16)
            y1_b = sbl("y1_b", [128, 256], BF16)
            aux_t = sbl("aux_t", [128, D], F32)
            t8 = sbl("t8", [128, 8], F32)
            ps_u1 = psl("ps_u1", [128, 512])
            ps_sc = psl("ps_sc", [128, 1024])
            ps_tr2 = [psl("ps_tr2a", [128, 128]), psl("ps_tr2b", [128, 128])]
            ps_y1 = psl("ps_y1", [128, 256])
            ps_ax = psl("ps_ax", [128, 512])
            ps_l = psl("ps_l", [128, 1])

            def phase_a2(s):
                d = s.do
                for dst, src in ((wu1_sb, wu1_h), (wvu_sb, wvu_h), (wqf_sb, wqf_h),
                                 (wu2_sb, wu2_h)):
                    d("gpsimd", lambda dst=dst, src=src: nc.gpsimd.dma_start(
                        out=dst[:], in_=src[:]), inc="dG", inc_n=16)
                d("sync", lambda: nc.sync.dma_start(out=ident_f2[:], in_=ident_h[:]),
                  inc="dS", inc_n=16)
                s.mark("a2w", "dG")
                d("sync", lambda: nc.sync.dma_start(out=lr_rep[:], in_=lrrep_h[:]),
                  inc="dS", inc_n=16)
                d("sync", lambda: nc.sync.dma_start(out=bu1_sb[:], in_=bu1r_h[:]),
                  inc="dS", inc_n=16)
                s.mark("a2c", "dS")

                for st in range(NST):
                    for jj in range(4):
                        w = []
                        if st >= 1 and jj == 0:
                            w.append(("tS", s.get(("stdone", st - 1))))
                        d("sync", lambda st=st, jj=jj: nc.sync.dma_start(
                            out=xn_st[:, jj * D:(jj + 1) * D], in_=xn_sp[st * 4 + jj]),
                            inc="dS", inc_n=16, waits=tuple(w))
                    s.mark(("xnld", st), "dS")
                    xn3 = xn_st[:].rearrange("p (j x) -> p j x", j=4)

                    for u in range(8):
                        w = [("dS", s.get(("xnld", st))), ("dG", s.get("a2w")),
                             ("dS", s.get("a2c"))]
                        if u >= 1:
                            w.append(("vS", s.get(("zev", st, u - 1))))
                        elif st >= 1:
                            w.append(("vS", s.get(("zev", st - 1, 7))))
                        for c in range(C):
                            d("tensor", lambda u=u, c=c: nc.tensor.matmul(
                                ps_u1[:],
                                wu1_sb[:, c * 1024 + u * 128: c * 1024 + (u + 1) * 128],
                                xn3[:, :, c * 128:(c + 1) * 128],
                                start=(c == 0), stop=(c == C - 1)),
                                inc=("tS" if c == C - 1 else None),
                                waits=tuple(w) if c == 0 else ())
                        s.mark(("u1m", st, u), "tS")
                        d("vector", lambda u=u: nc.vector.tensor_scalar(
                            zS[:, u * 512:(u + 1) * 512], ps_u1[:],
                            bu1_sb[:, u:u + 1], None, op0=ALU.add),
                            inc="vS", waits=(("tS", s.get(("u1m", st, u))),))
                        s.mark(("zev", st, u), "vS")
                    d("vector", lambda: nc.vector.tensor_mul(t2S[:], zS[:], zS[:]), inc="vS")
                    d("vector", lambda: nc.vector.tensor_scalar(
                        t2S[:], t2S[:], GC1, None, op0=ALU.mult), inc="vS")
                    d("vector", lambda: nc.vector.tensor_scalar(
                        t2S[:], t2S[:], GC0, None, op0=ALU.add), inc="vS")
                    d("vector", lambda: nc.vector.tensor_mul(t2S[:], t2S[:], zS[:]), inc="vS")
                    s.mark(("gel1", st), "vS")
                    d("scalar", lambda: nc.scalar.activation(t2S[:], t2S[:], AF.Tanh),
                      inc="sS", waits=(("vS", s.get(("gel1", st))),))
                    s.mark(("tanh", st), "sS")
                    d("vector", lambda: nc.vector.tensor_scalar(
                        zS[:], zS[:], 0.5, None, op0=ALU.mult), inc="vS")
                    d("vector", lambda: nc.vector.scalar_tensor_tensor(
                        gS[:], t2S[:], 1.0, zS[:], op0=ALU.add, op1=ALU.mult),
                        inc="vS", waits=(("sS", s.get(("tanh", st))),))
                    s.mark(("gbf", st), "vS")
                    for jj in range(4):
                        w = [("vS", s.get(("gbf", st)))]
                        if jj >= 1:
                            w.append(("sS", s.get(("sig", st, jj - 1))))
                        for u in range(8):
                            d("tensor", lambda u=u, jj=jj: nc.tensor.matmul(
                                ps_l[:],
                                gS[:, u * 512 + jj * 128: u * 512 + (jj + 1) * 128],
                                wu2_sb[:, u:u + 1],
                                start=(u == 0), stop=(u == 7)),
                                inc=("tS" if u == 7 else None),
                                waits=tuple(w) if u == 0 else ())
                        s.mark(("lm", st, jj), "tS")
                        d("vector", lambda: nc.vector.tensor_scalar(
                            small[:, 15:16], ps_l[:], float(bu2), None, op0=ALU.add),
                            inc="vS", waits=(("tS", s.get(("lm", st, jj))),))
                        s.mark(("lpre", st, jj), "vS")
                        d("scalar", lambda st=st, jj=jj: nc.scalar.activation(
                            sig_all[:, st * 4 + jj:st * 4 + jj + 1], small[:, 15:16],
                            AF.Sigmoid),
                            inc="sS", waits=(("vS", s.get(("lpre", st, jj))),))
                        s.mark(("sig", st, jj), "sS")

                    import os as _os6
                    if _os6.environ.get("KNOROUTER"):
                        s.mark(("stdone", st), "tS")
                        continue
                    for jj in range(4):
                        i = st * 4 + jj
                        w = [("dS", s.get(("xnld", st))), ("vS", s.get("rkdone")),
                             ("vS", s.get(("zev", st, 7)))]
                        if i >= 1:
                            w.append(("vS", s.get(("qfe", i - 1))))
                        for c in range(C):
                            d("tensor", lambda c=c, jj=jj: nc.tensor.matmul(
                                ps_u1[0:64, 0:128],
                                wqf_sb[:, c * 64:(c + 1) * 64],
                                xn_st[:, jj * D + c * 128: jj * D + (c + 1) * 128],
                                start=(c == 0), stop=(c == C - 1)),
                                inc=("tS" if c == C - 1 else None),
                                waits=tuple(w) if c == 0 else ())
                        s.mark(("qfm", i), "tS")
                        d("vector", lambda: nc.vector.tensor_copy(
                            qf_b[:], ps_u1[0:64, 0:128]),
                            inc="vS", waits=(("tS", s.get(("qfm", i))),))
                        s.mark(("qfe", i), "vS")
                        w = [("vS", s.get(("qfe", i)))]
                        if i >= 1:
                            w.append(("vS", s.get(("y1e", i - 1))))
                        d("tensor", lambda: nc.tensor.matmul(
                            ps_y1[0:48, 0:128], wr_sb[:], qf_b[:],
                            start=True, stop=True), inc="tS", waits=tuple(w))
                        s.mark(("rqm", i), "tS")
                        d("vector", lambda: nc.vector.tensor_copy(
                            wT_b[0:48, 0:128], ps_y1[0:48, 0:128]),
                            inc="vS", waits=(("tS", s.get(("rqm", i))),))
                        s.mark(("rqe", i), "vS")
                        for half in range(2):
                            w = [("vS", s.get(("rqe", i)))]
                            if i >= 1:
                                w.append(("vS", s.get(("zze", i - 1))))
                            d("tensor", lambda half=half: nc.tensor.matmul(
                                ps_sc[:, half * 512:(half + 1) * 512],
                                wT_b[0:48, 0:128],
                                rk_sb[0:48, half * 512:(half + 1) * 512],
                                start=True, stop=True),
                                inc=("tS" if half == 1 else None),
                                waits=tuple(w) if half == 0 else ())
                        s.mark(("scm", i), "tS")
                        for half in range(2):
                            d("vector", lambda half=half: nc.vector.scalar_tensor_tensor(
                                sc_tm[:, half * 512:(half + 1) * 512],
                                ps_sc[:, half * 512:(half + 1) * 512],
                                ISQ_R, lr_rep[:, half * 512:(half + 1) * 512],
                                op0=ALU.mult, op1=ALU.add),
                                inc="vS", waits=(("tS", s.get(("scm", i))),
                                                 ("dS", s.get("a2c"))))
                        s.mark(("sce", i), "vS")
                        for half in range(2):
                            d("tensor", lambda half=half: nc.tensor.matmul(
                                ps_sc[:, half * 512:(half + 1) * 512],
                                qf_b[:],
                                ak_sb[:, half * 512:(half + 1) * 512],
                                start=True, stop=True),
                                inc=("tS" if half == 1 else None),
                                waits=(("vS", s.get(("sce", i))),) if half == 0 else ())
                        s.mark(("a2m", i), "tS")
                        d("vector", lambda: nc.vector.max(out=t8[:], in_=sc_tm[:]),
                          inc="vS")
                        d("vector", lambda: nc.vector.tensor_scalar(
                            pen[:], sc_tm[:], t8[:, 7:8], None, op0=ALU.is_lt), inc="vS")
                        for half in range(2):
                            d("vector", lambda half=half: nc.vector.scalar_tensor_tensor(
                                zz[:, half * 512:(half + 1) * 512],
                                ps_sc[:, half * 512:(half + 1) * 512],
                                ISQ_B, lr_rep[:, half * 512:(half + 1) * 512],
                                op0=ALU.mult, op1=ALU.add),
                                inc="vS", waits=(("tS", s.get(("a2m", i))),))
                        s.mark(("zze", i), "vS")
                        d("vector", lambda: nc.vector.scalar_tensor_tensor(
                            zz[:], pen[:], -BIG, zz[:], op0=ALU.mult, op1=ALU.add),
                            inc="vS")
                        d("vector", lambda: nc.vector.reduce_max(
                            small[:, 9:10], zz[:], axis=AX, negate=True), inc="vS")
                        s.mark(("zmax", i), "vS")
                        d("scalar", lambda: nc.scalar.activation(
                            sc_tm[:], zz[:], AF.Exp, bias=small[:, 9:10]),
                            inc="sS", waits=(("vS", s.get(("zmax", i))),))
                        s.mark(("sexp", i), "sS")
                        d("vector", lambda: nc.vector.reduce_sum(
                            small[:, 10:11], sc_tm[:], axis=AX),
                            inc="vS", waits=(("sS", s.get(("sexp", i))),))
                        d("vector", lambda: nc.vector.reciprocal(
                            small[:, 11:12], small[:, 10:11]), inc="vS")
                        d("vector", lambda: nc.vector.tensor_scalar(
                            w_t[:], sc_tm[:], small[:, 11:12], None, op0=ALU.mult),
                            inc="vS")
                        s.mark(("wt", i), "vS")
                        import os as _os7
                        if _os7.environ.get("KNOWT"):
                            s.mark(("auxsp", i), "dS")
                            continue
                        for n in range(8):
                            ni = i * 8 + n
                            w = [("vS", s.get(("wt", i)))]
                            if ni >= 2:
                                w.append(("vS", s.get(("wte", ni - 2))))
                            d("tensor", lambda n=n: nc.tensor.transpose(
                                ps_tr2[n % 2][:],
                                w_t[:, n * 128:(n + 1) * 128], ident_f2[:]),
                                inc="tS", waits=tuple(w))
                            s.mark(("wtm", ni), "tS")
                            d("vector", lambda n=n: nc.vector.tensor_copy(
                                wT_b[:, n * 128:(n + 1) * 128],
                                ps_tr2[n % 2][:]),
                                inc="vS", waits=(("tS", s.get(("wtm", ni))),))
                            s.mark(("wte", ni), "vS")
                        s.mark(("wTd", i), "vS")
                        for half in range(2):
                            for n in range(8):
                                d("tensor", lambda half=half, n=n: nc.tensor.matmul(
                                    ps_y1[:, half * 128:(half + 1) * 128],
                                    vp_sb[:, n * 256 + half * 128: n * 256 + (half + 1) * 128],
                                    wT_b[:, n * 128:(n + 1) * 128],
                                    start=(n == 0), stop=(n == 7)),
                                    inc=("tS" if n == 7 else None),
                                    waits=(("vS", s.get(("wTd", i))),)
                                    if (half == 0 and n == 0) else ())
                        s.mark(("y1m", i), "tS")
                        d("vector", lambda: nc.vector.tensor_copy(y1_b[:], ps_y1[:]),
                          inc="vS", waits=(("tS", s.get(("y1m", i))),))
                        s.mark(("y1e", i), "vS")
                        for og in range(8):
                            w = [("vS", s.get(("y1e", i)))]
                            if og >= 1:
                                w.append(("vS", s.get(("axe", i, og - 1))))
                            ew = []
                            if og == 0 and i >= 1:
                                ew.append(("dS", s.get(("auxsp", i - 1))))
                            for half in range(2):
                                d("tensor", lambda og=og, half=half: nc.tensor.matmul(
                                    ps_ax[:],
                                    y1_b[:, half * 128:(half + 1) * 128],
                                    wvu_sb[:, half * D + og * 512: half * D + (og + 1) * 512],
                                    start=(half == 0), stop=(half == 1)),
                                    inc=("tS" if half == 1 else None),
                                    waits=tuple(w) if half == 0 else ())
                            s.mark(("axm", i, og), "tS")
                            d("vector", lambda og=og: nc.vector.tensor_copy(
                                aux_t[:, og * 512:(og + 1) * 512], ps_ax[:]),
                                inc="vS",
                                waits=tuple([("tS", s.get(("axm", i, og)))] + ew))
                            s.mark(("axe", i, og), "vS")
                        d("sync", lambda i=i: nc.sync.dma_start(
                            out=aux_sp[i], in_=aux_t[:]),
                            inc="dS", inc_n=16, waits=(("vS", s.get(("axe", i, 7))),))
                        s.mark(("auxsp", i), "dS")
                    s.mark(("stdone", st), "tS")
                s.barrier()

            if "2" in PH and not _os.environ.get("KA2B"):
                run_phase(phase_a2)

        with contextlib.ExitStack() as b1:
            sbl = lambda n, shp, dty: b1.enter_context(nc.sbuf_tensor(n, shp, dty))
            psl = lambda n, shp: b1.enter_context(nc.psum_tensor(n, shp, F32))
            xn_all = sbl("xn_all", [128, NT * D], BF16)
            wqh_sb = [sbl(f"wqh{i}", [128, C * 128], BF16) for i in range(2)]
            bq_sb = sbl("bq_sb", [128, QH], F32)
            exl = sbl("exl", [128, EW], F32)
            exa = sbl("exa", [128, EW], F32)
            qb = sbl("qb", [128, 512], F32)
            qm = sbl("qm", [128, 512], F32)
            qe = sbl("qe", [128, 512], F32)
            qphi = sbl("qphi", [128, 512], BF16)
            at_b = [sbl(f"at_b{i}", [128, 512], BF16) for i in range(2)]
            ones_sq = sbl("ones_sq", [128, 128], F32)
            ps_q = [psl(f"ps_q{i}", [128, 512]) for i in range(2)]
            ps_at = [psl(f"ps_at{i}", [128, 512]) for i in range(2)]
            ps_m = psl("ps_m", [128, 16])

            def phase_b1(s):
                d = s.do
                d("vector", lambda: nc.vector.memset(ones_sq[:], 1.0), inc="vS")
                d("sync", lambda: nc.sync.dma_start(out=bq_sb[:], in_=bqr_h[:]),
                  inc="dS", inc_n=16)
                s.mark("b1c", "dS")
                import os as _os3
                d("sync", lambda: nc.sync.dma_start(out=exl[:], in_=cc_out.ap()[0]),
                  inc="dS", inc_n=16,
                  waits=(("cc", 1),) if not _os3.environ.get("KNOCC") else ())
                s.mark("ex0", "dS")
                d("vector", lambda: nc.vector.tensor_scalar(
                    exa[:], exl[:], bmask_sb[:, 0:1], None, op0=ALU.mult),
                    inc="vS", waits=(("dS", s.get("ex0")),))
                d("vector", lambda: nc.vector.tensor_copy(
                    small[:, 0:1], exl[:, 1032:1033]), inc="vS")
                s.mark(("exc", 0), "vS")
                for b in range(1, 4):
                    d("sync", lambda b=b: nc.sync.dma_start(
                        out=exl[:], in_=cc_out.ap()[b]),
                        inc="dS", inc_n=16, waits=(("vS", s.get(("exc", b - 1))),))
                    s.mark(("exb", b), "dS")
                    d("vector", lambda b=b: nc.vector.scalar_tensor_tensor(
                        exa[:], exl[:], bmask_sb[:, b:b + 1], exa[:],
                        op0=ALU.mult, op1=ALU.add),
                        inc="vS", waits=(("dS", s.get(("exb", b))),))
                    d("vector", lambda b=b: nc.vector.tensor_copy(
                        small[:, b:b + 1], exl[:, 1032:1033]), inc="vS")
                    s.mark(("exc", b), "vS")
                d("vector", lambda: nc.vector.tensor_copy(kv_bf[:], exa[:, 0:1024]),
                  inc="vS")
                d("vector", lambda: nc.vector.tensor_scalar(
                    exa[:, 1024:1032], exa[:, 1024:1032], 1e-10, None, op0=ALU.add),
                    inc="vS")
                d("vector", lambda: nc.vector.reciprocal(
                    inv_ks[:], exa[:, 1024:1032]), inc="vS")
                s.mark("kvdone", "vS")
                d("vector", lambda: nc.vector.reduce_sum(
                    small[:, 4:5], small[:, 0:4], axis=AX), inc="vS")
                s.mark("lvrow", "vS")
                d("tensor", lambda: nc.tensor.matmul(
                    ps_m[:, 0:1], ones_sq[:], small[:, 4:5],
                    start=True, stop=True),
                    inc="tS", waits=(("vS", s.get("lvrow")),))
                s.mark("lvbc", "tS")
                d("vector", lambda: nc.vector.tensor_scalar(
                    small[:, 6:7], ps_m[:, 0:1], 1.0 / 16384.0, 1e-6,
                    op0=ALU.mult, op1=ALU.add),
                    inc="vS", waits=(("tS", s.get("lvbc")),))
                d("vector", lambda: nc.vector.reciprocal(invmean[:], small[:, 6:7]),
                  inc="vS")
                s.mark("invmean", "vS")

                for j in range(NT):
                    d("sync", lambda j=j: nc.sync.dma_start(
                        out=xn_all[:, j * D:(j + 1) * D], in_=xn_sp[j]),
                        inc="dS", inc_n=16)
                s.mark("xnall", "dS")
                xnr = xn_all[:].rearrange("p (j x) -> p j x", j=NT)

                d("gpsimd", lambda: nc.gpsimd.dma_start(
                    out=wqh_sb[0][:], in_=wq_h[0]), inc="dG", inc_n=16)
                s.mark(("wql", 0), "dG")

                for h in range(QH):
                    if h + 1 < QH:
                        w = []
                        if h >= 1:
                            w.append(("tS", s.get(("qdone", h - 1))))
                        d("gpsimd", lambda h=h: nc.gpsimd.dma_start(
                            out=wqh_sb[(h + 1) % 2][:], in_=wq_h[h + 1]),
                            inc="dG", inc_n=16, waits=tuple(w))
                        s.mark(("wql", h + 1), "dG")
                    for st in range(NST):
                        idx = h * NST + st
                        pq = ps_q[st % 2]
                        pa = ps_at[st % 2]
                        w = [("dG", s.get(("wql", h))), ("dS", s.get("xnall"))]
                        if idx >= 2:
                            w.append(("vS", s.get(("qbe", idx - 2))))
                        for c in range(C):
                            d("tensor", lambda c=c, h=h, st=st, pq=pq: nc.tensor.matmul(
                                pq[:],
                                wqh_sb[h % 2][:, c * 128:(c + 1) * 128],
                                xnr[:, 4 * st:4 * st + 4, c * 128:(c + 1) * 128],
                                start=(c == 0), stop=(c == C - 1)),
                                inc=("tS" if c == C - 1 else None),
                                waits=tuple(w) if c == 0 else ())
                        s.mark(("qmm", idx), "tS")
                        w = [("tS", s.get(("qmm", idx))), ("dS", s.get("b1c"))]
                        if idx >= 1:
                            w.append(("tS", s.get(("atm", idx - 1))))
                        d("vector", lambda pq=pq, h=h: nc.vector.tensor_scalar(
                            qb[:], pq[:], bq_sb[:, h:h + 1], None, op0=ALU.add),
                            inc="vS", waits=tuple(w))
                        s.mark(("qbe", idx), "vS")
                        d("vector", lambda: nc.vector.tensor_scalar(
                            qm[:], qb[:], 0.0, None, op0=ALU.min), inc="vS")
                        s.mark(("qmi", idx), "vS")
                        d("scalar", lambda: nc.scalar.activation(qe[:], qm[:], AF.Exp),
                          inc="sS", waits=(("vS", s.get(("qmi", idx))),))
                        s.mark(("qex", idx), "sS")
                        d("vector", lambda: nc.vector.scalar_tensor_tensor(
                            qphi[:], qb[:], 1.0, qe[:], op0=ALU.add, op1=ALU.max),
                            inc="vS", waits=(("sS", s.get(("qex", idx))),
                                             ("vS", s.get("kvdone"))))
                        s.mark(("qph", idx), "vS")
                        w = [("vS", s.get(("qph", idx)))]
                        if idx >= 2:
                            w.append(("vS", s.get(("ate", idx - 2))))
                        d("tensor", lambda h=h, pa=pa: nc.tensor.matmul(
                            pa[:], kv_bf[:, (h // 4) * 128:(h // 4 + 1) * 128],
                            qphi[:], start=True, stop=True),
                            inc="tS", waits=tuple(w))
                        s.mark(("atm", idx), "tS")
                        ab = at_b[idx % 2]
                        w = [("tS", s.get(("atm", idx)))]
                        if idx >= 2:
                            w.append(("dS", s.get(("atsp", idx - 2))))
                        d("vector", lambda pa=pa, h=h, ab=ab: nc.vector.tensor_scalar(
                            ab[:], pa[:], inv_ks[:, h // 4:h // 4 + 1], None,
                            op0=ALU.mult),
                            inc="vS", waits=tuple(w))
                        s.mark(("ate", idx), "vS")
                        d("sync", lambda h=h, st=st, ab=ab: nc.sync.dma_start(
                            out=attn_sp[:, h * T + st * 512: h * T + (st + 1) * 512],
                            in_=ab[:]),
                            inc="dS", inc_n=16, waits=(("vS", s.get(("ate", idx))),))
                        s.mark(("atsp", idx), "dS")
                    s.mark(("qdone", h), "tS")
                s.barrier()

            if "3" in PH:
                run_phase(phase_b1)

        with contextlib.ExitStack() as b2:
            sbl = lambda n, shp, dty: b2.enter_context(nc.sbuf_tensor(n, shp, dty))
            psl = lambda n, shp: b2.enter_context(nc.psum_tensor(n, shp, F32))
            attn_all = sbl("attn_all", [128, QH * T], BF16)
            wo_sb = sbl("wo_sb", [128, C * 512], BF16)
            bo_rep = sbl("bo_rep", [128, D], F32)
            x_sl = [sbl(f"x_sl{i}", [128, 512], F32) for i in range(2)]
            aux_sl = [sbl(f"aux_sl{i}", [128, 512], F32) for i in range(2)]
            xn_sl = sbl("xn_sl", [128, 512], F32)
            s_sb = sbl("s_sb", [128, 512], F32)
            g_sl = sbl("g_sl", [128, 512], F32)
            o_sl = [sbl(f"o_sl{i}", [128, 512], F32) for i in range(2)]
            ps_s = [psl(f"ps_s{i}", [128, 512]) for i in range(2)]

            def phase_b2(s):
                d = s.do
                d("sync", lambda: nc.sync.dma_start(out=bo_rep[:], in_=borep_h[:]),
                  inc="dS", inc_n=16)
                s.mark("b2c", "dS")
                QT4 = QH * T // 4
                for q in range(4):
                    d("sync", lambda q=q: nc.sync.dma_start(
                        out=attn_all[:, q * QT4:(q + 1) * QT4],
                        in_=attn_sp[:, q * QT4:(q + 1) * QT4]),
                        inc="dS", inc_n=16)
                s.mark("atall", "dS")
                d("vector", lambda: nc.vector.tensor_scalar(
                    gate_all[:], lv_all[:], invmean[:, 0:1], None, op0=ALU.mult),
                    inc="vS")
                d("vector", lambda: nc.vector.tensor_scalar(
                    gate_all[:], gate_all[:], 0.5, None, op0=ALU.mult), inc="vS")
                d("vector", lambda: nc.vector.scalar_tensor_tensor(
                    gate_all[:], sig_all[:], 2.5, gate_all[:],
                    op0=ALU.mult, op1=ALU.add), inc="vS")
                d("vector", lambda: nc.vector.tensor_scalar(
                    gate_all[:], gate_all[:], 0.0, None, op0=ALU.max), inc="vS")
                d("vector", lambda: nc.vector.tensor_scalar(
                    gate_all[:], gate_all[:], 5.0, None, op0=ALU.min), inc="vS")
                d("vector", lambda: nc.vector.tensor_scalar(
                    gate_all[:], gate_all[:], 0.5, None, op0=ALU.subtract), inc="vS")
                d("vector", lambda: nc.vector.tensor_scalar(
                    gate_all[:], gate_all[:], 1.0 / 1.5, None, op0=ALU.mult), inc="vS")
                d("vector", lambda: nc.vector.tensor_scalar(
                    gate_all[:], gate_all[:], 0.0, None, op0=ALU.max), inc="vS")
                d("vector", lambda: nc.vector.tensor_scalar(
                    gate_all[:], gate_all[:], 1.0, None, op0=ALU.min), inc="vS")
                d("vector", lambda: nc.vector.tensor_scalar(
                    gate_all[:], gate_all[:], float(gw1), None, op0=ALU.mult), inc="vS")
                d("vector", lambda: nc.vector.tensor_scalar(
                    gate_all[:], gate_all[:], float(gbias), None, op0=ALU.add), inc="vS")
                s.mark("un", "vS")
                d("scalar", lambda: nc.scalar.activation(
                    gate_all[:], gate_all[:], AF.Sigmoid),
                    inc="sS", waits=(("vS", s.get("un")),))
                s.mark("gsig", "sS")
                d("vector", lambda: nc.vector.tensor_scalar(
                    lv_all[:], gate_all[:], 0.05, None, op0=ALU.is_ge),
                    inc="vS", waits=(("sS", s.get("gsig")),))
                d("vector", lambda: nc.vector.tensor_mul(
                    gate_all[:], gate_all[:], lv_all[:]), inc="vS")
                s.mark("gate", "vS")

                for og in range(8):
                    w = []
                    if og >= 1:
                        w.append(("tS", s.get(("ogdone", og - 1))))
                    d("gpsimd", lambda og=og: nc.gpsimd.dma_start(
                        out=wo_sb[:], in_=wo_h[og]),
                        inc="dG", inc_n=16, waits=tuple(w))
                    s.mark(("wol", og), "dG")
                    for t in range(NT):
                        idx = og * NT + t
                        pss = ps_s[t % 2]
                        w = [("dG", s.get(("wol", og))), ("dS", s.get("atall"))]
                        if idx >= 2:
                            w.append(("vS", s.get(("sev", idx - 2))))
                        for a in range(C):
                            d("tensor", lambda a=a, t=t, pss=pss: nc.tensor.matmul(
                                pss[:],
                                attn_all[:, a * T + t * 128: a * T + (t + 1) * 128],
                                wo_sb[:, a * 512:(a + 1) * 512],
                                start=(a == 0), stop=(a == C - 1)),
                                inc=("tS" if a == C - 1 else None),
                                waits=tuple(w) if a == 0 else ())
                        s.mark(("smm", idx), "tS")
                        xb = x_sl[t % 2]
                        ab = aux_sl[t % 2]
                        w = []
                        if idx >= 2:
                            w.append(("vS", s.get(("used", idx - 2))))
                        d("sync", lambda t=t, og=og, xb=xb: nc.sync.dma_start(
                            out=xb[:], in_=x_h[t * 128:(t + 1) * 128,
                                              og * 512:(og + 1) * 512]),
                            inc="dS", inc_n=16, waits=tuple(w))
                        d("sync", lambda t=t, og=og, ab=ab: nc.sync.dma_start(
                            out=ab[:], in_=aux_sp[t][:, og * 512:(og + 1) * 512]),
                            inc="dS", inc_n=16)
                        s.mark(("cld", idx), "dS")
                        d("vector", lambda pss=pss, og=og: nc.vector.tensor_add(
                            s_sb[:], pss[:], bo_rep[:, og * 512:(og + 1) * 512]),
                            inc="vS", waits=(("tS", s.get(("smm", idx))),
                                             ("dS", s.get("b2c"))))
                        s.mark(("sev", idx), "vS")
                        w = [("dS", s.get(("cld", idx))), ("vS", s.get("gate"))]
                        if idx >= 1:
                            w.append(("vS", s.get(("ffn", idx - 1))))
                        d("scalar", lambda t=t, xb=xb: nc.scalar.activation(
                            xn_sl[:], xb[:], AF.Identity,
                            bias=nmr_all[:, t:t + 1], scale=rstd_all[:, t:t + 1]),
                            inc="sS", waits=tuple(w))
                        s.mark(("xnsl", idx), "sS")
                        d("vector", lambda t=t, ab=ab: nc.vector.scalar_tensor_tensor(
                            g_sl[:], ab[:], gate_all[:, t:t + 1], xn_sl[:],
                            op0=ALU.mult, op1=ALU.add),
                            inc="vS", waits=(("sS", s.get(("xnsl", idx))),))
                        s.mark(("ffn", idx), "vS")
                        d("scalar", lambda: nc.scalar.activation(
                            g_sl[:], g_sl[:], AF.Silu),
                            inc="sS", waits=(("vS", s.get(("ffn", idx))),))
                        s.mark(("silu", idx), "sS")
                        ob = o_sl[t % 2]
                        w = [("sS", s.get(("silu", idx)))]
                        if idx >= 2:
                            w.append(("dS", s.get(("odma", idx - 2))))
                        d("vector", lambda ob=ob: nc.vector.tensor_mul(
                            ob[:], g_sl[:], s_sb[:]),
                            inc="vS", waits=tuple(w))
                        d("vector", lambda ob=ob, xb=xb: nc.vector.tensor_add(
                            ob[:], ob[:], xb[:]), inc="vS")
                        s.mark(("used", idx), "vS")
                        d("sync", lambda t=t, og=og, ob=ob: nc.sync.dma_start(
                            out=out_h[t * 128:(t + 1) * 128, og * 512:(og + 1) * 512],
                            in_=ob[:]),
                            inc="dS", inc_n=16, waits=(("vS", s.get(("used", idx))),))
                        s.mark(("odma", idx), "dS")
                    s.mark(("ogdone", og), "tS")
                d("sync", lambda: nc.sync.nop(), waits=(("dS", s.cnt["dS"]),))

            if "4" in PH:
                run_phase(phase_b2)
            else:
                def phase_dummy(s):
                    s.do("vector", lambda: nc.vector.memset(o_sl[0][:], 0.0), inc="vS")
                    for t in range(NT):
                        for og in range(8):
                            s.do("sync", lambda t=t, og=og: nc.sync.dma_start(
                                out=out_h[t * 128:(t + 1) * 128, og * 512:(og + 1) * 512],
                                in_=o_sl[0][:]),
                                inc="dS", inc_n=16,
                                waits=(("vS", s.cnt["vS"]),) if (t == 0 and og == 0) else ())
                    s.do("sync", lambda: nc.sync.dma_start(
                        out=out_h[0:128, 0:EW], in_=cc_out.ap()[0]),
                        inc="dS", inc_n=16, waits=(("dS", s.cnt["dS"]),))
                    s.do("gpsimd", lambda: nc.gpsimd.dma_start(
                        out=out_h[384:512, :], in_=xn_sp[0]), inc="dG", inc_n=16,
                        waits=(("dS", s.cnt["dS"]),))
                    s.do("sync", lambda: nc.sync.dma_start(
                        out=out_h[640:768, :], in_=aux_sp[0]), inc="dS", inc_n=16)
                    for col, tile in ((16, rstd_all), (32, nmr_all), (48, lv_all),
                                      (64, sig_all), (80, gate_all)):
                        s.do("sync", lambda col=col, tile=tile: nc.sync.dma_start(
                            out=out_h[256:384, col:col + NT], in_=tile[:]),
                            inc="dS", inc_n=16)
                    s.do("gpsimd", lambda: nc.gpsimd.dma_start(
                        out=out_h[512:640, 0:2048], in_=attn_sp[:, 0:2048]),
                        inc="dG", inc_n=16)
                    s.do("sync", lambda: nc.sync.nop(), waits=(("dS", s.cnt["dS"]),))
                    s.do("gpsimd", lambda: nc.gpsimd.nop(), waits=(("dG", s.cnt["dG"]),))
                run_phase(phase_dummy)

    return nc


def kernel(**inp):
    x = np.ascontiguousarray(np.asarray(inp["L0_attn_hidden_states"], dtype=np.float32))
    f32 = lambda a: np.ascontiguousarray(np.asarray(a, dtype=np.float32))

    wqT = f32(inp["L0_attn_Wq"]).T
    wkT = f32(inp["L0_attn_Wk"]).T
    wvT = f32(inp["L0_attn_Wv"]).T
    woT = f32(inp["L0_attn_Wo"]).T
    wqfT = f32(inp["L0_ffn_Wq"]).T
    wrT = f32(inp["L0_ffn_Wr"]).T
    akT = np.ascontiguousarray(f32(inp["L0_ffn_aux_keys"]).T)
    avT = f32(inp["L0_ffn_aux_values"]).T
    wvdT = f32(inp["L0_ffn_Wvd"]).T
    wvuT = f32(inp["L0_ffn_Wvu"]).T
    wu1T = f32(inp["L0_ffn_Wu1"]).T
    wu2 = f32(inp["L0_ffn_Wu2"])

    def chunk_pc(wT, width):
        return np.ascontiguousarray(
            wT.reshape(C, 128, width).transpose(1, 0, 2).reshape(128, C * width))

    wk_host = chunk_pc(wkT, 1024)
    wv_host = chunk_pc(wvT, 1024)
    wqf_host = chunk_pc(wqfT, 64)
    av_host = chunk_pc(avT, 1024)
    wvd_host = chunk_pc(wvdT, 256)
    wu1_host = chunk_pc(wu1T, 1024)
    wq_host = np.ascontiguousarray(
        wqT.reshape(C, 128, QH, 128).transpose(2, 1, 0, 3).reshape(QH, 128, C * 128))
    wo_host = np.ascontiguousarray(
        woT.reshape(C, 128, 8, 512).transpose(2, 1, 0, 3).reshape(8, 128, C * 512))
    wvu_host = np.ascontiguousarray(
        wvuT.reshape(2, 128, D).transpose(1, 0, 2).reshape(128, 2 * D))
    wu2_host = np.ascontiguousarray(wu2.reshape(8, 128).T)
    bq_r = np.ascontiguousarray(f32(inp["L0_attn_bq"]).reshape(QH, 128).T)
    bu1_r = np.ascontiguousarray(f32(inp["L0_ffn_bu1"]).reshape(8, 128).T)
    rep = lambda v, n: np.ascontiguousarray(
        np.broadcast_to(f32(v).reshape(1, n), (128, n)))
    bk_rep = rep(inp["L0_attn_bk"], 1024)
    bv_rep = rep(inp["L0_attn_bv"], 1024)
    bo_rep = rep(inp["L0_attn_bo"], D)
    lr_rep = rep(inp["L0_ffn_log_rel"], NSLOT)
    ident = np.eye(128, dtype=np.float32)

    gw1 = float(np.asarray(inp["L0_ffn_gate_w1"]))
    gbias = float(np.asarray(inp["L0_ffn_gate_bias"]))
    bu2 = float(np.asarray(inp["L0_ffn_bu2"]).reshape(-1)[0])

    nc = build_program(gw1, gbias, bu2)

    shared = dict(wk=wk_host, wv=wv_host, wq=wq_host, wo=wo_host,
                  wqf=wqf_host, wr=wrT, ak=akT, av=av_host, wvd=wvd_host,
                  wvu=wvu_host, wu1=wu1_host, wu2=wu2_host, bqr=bq_r,
                  bu1r=bu1_r, bkrep=bk_rep, bvrep=bv_rep, borep=bo_rep,
                  lrrep=lr_rep, ident=ident)
    in_maps = []
    for core in range(NCORES):
        b, half = core // 2, core % 2
        bm = np.zeros((128, 4), dtype=np.float32)
        bm[:, b] = 1.0
        m = dict(shared)
        m["x"] = np.ascontiguousarray(x[b, half * T:(half + 1) * T, :])
        m["bmask"] = bm
        in_maps.append(m)

    res = run_bass_kernel_spmd(nc, in_maps, core_ids=list(range(NCORES)))

    out = np.empty((4, 4096, D), dtype=np.float32)
    for core in range(NCORES):
        b, half = core // 2, core % 2
        out[b, half * T:(half + 1) * T, :] = res.results[core]["out"]
    return out


# version 08
# speedup vs baseline: 1.1055x; 1.1055x over previous
import contextlib
import numpy as np
import concourse.bass as bass
import concourse.mybir as mybir
from concourse.bass_utils import run_bass_kernel_spmd

F32 = mybir.dt.float32
BF16 = mybir.dt.bfloat16
AX = mybir.AxisListType.X
AF = mybir.ActivationFunctionType
ALU = mybir.AluOpType

NCORES = 8
T = 2048
NT = 16
NST = 4
D = 4096
C = 32
KH = 8
QH = 32
NSLOT = 1024
ISQ_R = float(1.0 / np.sqrt(48.0))
ISQ_B = 0.125
BIG = 1.0e30
GC0 = 0.7978845608028654
GC1 = GC0 * 0.044715
EW = 1040

ENGS = ("sync", "gpsimd", "vector", "scalar", "tensor")


class Sched:

    def __init__(self, which, eng, sems, cnt, rec):
        self.which = which
        self.e = eng
        self.sems = sems
        self.cnt = cnt
        self.rec = rec

    def do(self, eng, emit, inc=None, inc_n=1, waits=()):
        if self.which == eng:
            if eng in ("vector", "scalar") and inc is not None and self.cnt[inc] > 0:
                self.e.wait_ge(self.sems[inc], self.cnt[inc])
            for (sname, val) in waits:
                if val is not None and val > 0:
                    self.e.wait_ge(self.sems[sname], val)
            inst = emit()
            if inc is not None:
                inst.then_inc(self.sems[inc], inc_n)
        if inc is not None:
            self.cnt[inc] += inc_n

    def mark(self, key, semname):
        self.rec[key] = self.cnt[semname]

    def get(self, key):
        return self.rec[key]

    def barrier(self):
        base = self.cnt["bY"]
        for j, e in enumerate(ENGS):
            if self.which == e:
                if base + j > 0:
                    self.e.wait_ge(self.sems["bY"], base + j)
                self.e.nop().then_inc(self.sems["bY"], 1)
            self.cnt["bY"] += 1
        if self.which is not None and self.which != ENGS[-1]:
            self.e.wait_ge(self.sems["bY"], base + len(ENGS))


def build_program(gw1, gbias, bu2):
    import os as _os
    PH = _os.environ.get("KPHASES", "1234")
    nc = bass.Bass("TRN2", target_bir_lowering=False)
    dt = nc.dram_tensor

    x_h = dt("x", [T, D], F32, kind="ExternalInput").ap()
    out_h = dt("out", [T, D], F32, kind="ExternalOutput").ap()
    wk_h = dt("wk", [128, C * 1024], F32, kind="ExternalInput").ap()
    wv_h = dt("wv", [128, C * 1024], F32, kind="ExternalInput").ap()
    wq_h = dt("wq", [QH, 128, C * 128], F32, kind="ExternalInput").ap()
    wo_h = dt("wo", [8, 128, C * 512], F32, kind="ExternalInput").ap()
    wqf_h = dt("wqf", [128, C * 64], F32, kind="ExternalInput").ap()
    wr_h = dt("wr", [64, 48], F32, kind="ExternalInput").ap()
    ak_h = dt("ak", [64, NSLOT], F32, kind="ExternalInput").ap()
    av_h = dt("av", [128, C * 1024], F32, kind="ExternalInput").ap()
    wvd_h = dt("wvd", [128, C * 256], F32, kind="ExternalInput").ap()
    wvu_h = dt("wvu", [128, 2 * D], F32, kind="ExternalInput").ap()
    wu1_h = dt("wu1", [128, C * 1024], F32, kind="ExternalInput").ap()
    wu2_h = dt("wu2", [128, 8], F32, kind="ExternalInput").ap()
    bqr_h = dt("bqr", [128, QH], F32, kind="ExternalInput").ap()
    bu1r_h = dt("bu1r", [128, 8], F32, kind="ExternalInput").ap()
    bkrep_h = dt("bkrep", [128, 1024], F32, kind="ExternalInput").ap()
    bvrep_h = dt("bvrep", [128, 1024], F32, kind="ExternalInput").ap()
    borep_h = dt("borep", [128, D], F32, kind="ExternalInput").ap()
    lrrep_h = dt("lrrep", [128, NSLOT], F32, kind="ExternalInput").ap()
    ident_h = dt("ident", [128, 128], F32, kind="ExternalInput").ap()
    bmask_h = dt("bmask", [128, 4], F32, kind="ExternalInput").ap()

    xn_sp = dt("xn_sp", [NT, 128, D], BF16, kind="Internal").ap()
    attn_sp = dt("attn_sp", [128, QH * T], BF16, kind="Internal").ap()
    aux_sp = dt("aux_sp", [NT, 128, D], F32, kind="Internal").ap()
    cc_in = dt("cc_in", [4, 128, EW], F32, kind="Internal")
    cc_out = dt("cc_out", [4, 128, EW], F32, kind="Internal")

    es = contextlib.ExitStack()
    sbt = lambda name, shp, dtype: es.enter_context(nc.sbuf_tensor(name, shp, dtype))

    with es:
        rstd_all = sbt("rstd_all", [128, NT], F32)
        nmr_all = sbt("nmr_all", [128, NT], F32)
        lv_all = sbt("lv_all", [128, NT], F32)
        sig_all = sbt("sig_all", [128, NT], F32)
        gate_all = sbt("gate_all", [128, NT], F32)
        kv_bf = sbt("kv_bf", [128, KH * 128], BF16)
        inv_ks = sbt("inv_ks", [128, KH], F32)
        invmean = sbt("invmean", [128, 1], F32)
        ones_f = sbt("ones_f", [128, 1], F32)
        ones_b = sbt("ones_b", [128, 1], BF16)
        small = sbt("small", [128, 16], F32)
        bmask_sb = sbt("bmask_sb", [128, 4], F32)

        sems = {n: es.enter_context(nc.semaphore(n)) for n in
                ("dS", "dG", "tS", "vS", "sS", "bY", "bG", "cc")}

        cnt_g = {n: 0 for n in sems}
        rec = {}

        def run_phase(body):
            start = dict(cnt_g)
            final = {}

            with nc.Block() as block:
                def emit(which, eng):
                    local = dict(start)
                    body(Sched(which, eng, sems, local, rec))
                    final.update(local)

                @block.sync
                def _(e):
                    emit("sync", e)

                @block.gpsimd
                def _(e):
                    emit("gpsimd", e)

                @block.vector
                def _(e):
                    emit("vector", e)

                @block.scalar
                def _(e):
                    emit("scalar", e)

                @block.tensor
                def _(e):
                    emit("tensor", e)

            cnt_g.clear()
            cnt_g.update(final)

        with contextlib.ExitStack() as a1:
            sbl = lambda n, shp, dty: a1.enter_context(nc.sbuf_tensor(n, shp, dty))
            psl = lambda n, shp: a1.enter_context(nc.psum_tensor(n, shp, F32))
            wk_sb = sbl("wk_sb", [128, C * 1024], BF16)
            wv_sb = sbl("wv_sb", [128, C * 1024], BF16)
            x_t = [sbl("x_t0", [128, D], F32)]
            xn_t = sbl("xn_t", [128, D], F32)
            xn_fm = sbl("xn_fm", [128, D], BF16)
            k_bf = sbl("k_bf", [128, 1024], BF16)
            v_bf = sbl("v_bf", [128, 1024], BF16)
            bk_rep = sbl("bk_rep", [128, 1024], F32)
            bv_rep = sbl("bv_rep", [128, 1024], F32)
            ident_f = sbl("ident_f", [128, 128], F32)
            scr0 = sbl("scr0", [128, EW], F32)
            scr1 = sbl("scr1", [128, 1024], F32)
            exch = sbl("exch", [128, EW], F32)
            ps_tr = [psl(f"ps_tr{i}", [128, 512]) for i in range(2)]
            ps_gen = psl("ps_gen", [128, 1024])
            ps_kv = psl("ps_kv", [128, 1024])
            ps_ks = psl("ps_ks", [128, 8])

            def phase_a1(s):
                d = s.do
                d("gpsimd", lambda: nc.gpsimd.dma_start(out=wk_sb[:], in_=wk_h[:]), inc="dG", inc_n=16)
                d("gpsimd", lambda: nc.gpsimd.dma_start(out=wv_sb[:], in_=wv_h[:]), inc="dG", inc_n=16)
                s.mark("wkv", "dG")
                d("sync", lambda: nc.sync.dma_start(out=bk_rep[:], in_=bkrep_h[:]), inc="dS", inc_n=16)
                d("sync", lambda: nc.sync.dma_start(out=bv_rep[:], in_=bvrep_h[:]), inc="dS", inc_n=16)
                d("sync", lambda: nc.sync.dma_start(out=ident_f[:], in_=ident_h[:]), inc="dS", inc_n=16)
                d("sync", lambda: nc.sync.dma_start(out=bmask_sb[:], in_=bmask_h[:]), inc="dS", inc_n=16)
                s.mark("a1c", "dS")
                d("vector", lambda: nc.vector.memset(ones_f[:], 1.0), inc="vS")
                d("vector", lambda: nc.vector.memset(ones_b[:], 1.0), inc="vS")
                d("vector", lambda: nc.vector.memset(exch[:], 0.0), inc="vS")
                s.mark("exch0", "vS")
                for b in range(4):
                    d("gpsimd", lambda b=b: nc.gpsimd.dma_start(
                        out=cc_in.ap()[b], in_=exch[:]),
                        inc="dG", inc_n=16, waits=(("vS", s.get("exch0")),))
                s.mark("ccz", "dG")

                for i in range(NT):
                    xb = x_t[0]
                    sqc = 13 + (i % 2)
                    w = []
                    if i >= 1:
                        w += [("sS", s.get(("xnact", i - 1))),
                              ("vS", s.get(("xsum", i - 1)))]
                    d("sync", lambda i=i, xb=xb: nc.sync.dma_start(
                        out=xb[:], in_=x_h[i * 128:(i + 1) * 128, :]),
                        inc="dS", inc_n=16, waits=tuple(w))
                    s.mark(("xld", i), "dS")

                    d("vector", lambda xb=xb: nc.vector.reduce_sum(
                        small[:, 0:1], xb[:], axis=AX),
                        inc="vS", waits=(("dS", s.get(("xld", i))),))
                    s.mark(("xsum", i), "vS")
                    w = [("dS", s.get(("xld", i)))]
                    if i >= 1:
                        w.append(("tS", s.get(("trdone", i - 1))))
                    if i >= 2:
                        w.append(("vS", s.get(("varred", i - 2))))
                    d("scalar", lambda xb=xb, sqc=sqc: nc.scalar.activation(
                        xn_t[:], xb[:], AF.Square, accum_out=small[:, sqc:sqc + 1]),
                        inc="sS", waits=tuple(w))
                    s.mark(("ssq", i), "sS")
                    d("vector", lambda: nc.vector.tensor_scalar(
                        small[:, 2:3], small[:, 0:1], 1.0 / D, None, op0=ALU.mult), inc="vS")
                    d("vector", lambda: nc.vector.tensor_mul(
                        small[:, 3:4], small[:, 2:3], small[:, 2:3]), inc="vS")
                    d("vector", lambda sqc=sqc: nc.vector.scalar_tensor_tensor(
                        small[:, 4:5], small[:, sqc:sqc + 1], 1.0 / D, small[:, 3:4],
                        op0=ALU.mult, op1=ALU.subtract),
                        inc="vS", waits=(("sS", s.get(("ssq", i))),))
                    s.mark(("varred", i), "vS")
                    d("vector", lambda: nc.vector.tensor_scalar(
                        small[:, 5:6], small[:, 4:5], 1e-5, None, op0=ALU.add), inc="vS")
                    s.mark(("veps", i), "vS")
                    d("scalar", lambda: nc.scalar.activation(
                        small[:, 6:7], small[:, 5:6], AF.Ln),
                        inc="sS", waits=(("vS", s.get(("veps", i))),))
                    d("scalar", lambda i=i: nc.scalar.activation(
                        rstd_all[:, i:i + 1], small[:, 6:7], AF.Exp, scale=-0.5), inc="sS")
                    s.mark(("rstd", i), "sS")
                    d("vector", lambda i=i: nc.vector.scalar_tensor_tensor(
                        nmr_all[:, i:i + 1], small[:, 2:3], -1.0, rstd_all[:, i:i + 1],
                        op0=ALU.mult, op1=ALU.mult),
                        inc="vS", waits=(("sS", s.get(("rstd", i))),))
                    s.mark(("nmr", i), "vS")
                    d("vector", lambda: nc.vector.reciprocal(
                        small[:, 7:8], small[:, 5:6]), inc="vS")
                    d("vector", lambda: nc.vector.tensor_mul(
                        small[:, 8:9], small[:, 4:5], small[:, 7:8]), inc="vS")
                    s.mark(("ratio", i), "vS")
                    d("scalar", lambda i=i: nc.scalar.activation(
                        lv_all[:, i:i + 1], small[:, 8:9], AF.Ln, bias=1.0),
                        inc="sS", waits=(("vS", s.get(("ratio", i))),))
                    d("scalar", lambda i=i, xb=xb: nc.scalar.activation(
                        xn_t[:], xb[:], AF.Identity,
                        bias=nmr_all[:, i:i + 1], scale=rstd_all[:, i:i + 1]),
                        inc="sS", waits=(("vS", s.get(("nmr", i))),))
                    s.mark(("xnact", i), "sS")

                    for g in range(8):
                        gi = i * 8 + g
                        w = [("sS", s.get(("xnact", i)))]
                        if gi >= 2:
                            w.append(("sS", s.get(("tre", gi - 2))))
                        for q in range(4):
                            cq = 4 * g + q
                            d("tensor", lambda g=g, q=q, cq=cq: nc.tensor.transpose(
                                ps_tr[g % 2][:, q * 128:(q + 1) * 128],
                                xn_t[:, cq * 128:(cq + 1) * 128], ident_f[:]),
                                inc=("tS" if q == 3 else None),
                                waits=tuple(w) if q == 0 else ())
                        s.mark(("trw", gi), "tS")
                        w = [("tS", s.get(("trw", gi)))]
                        if g == 0 and i >= 1:
                            w.append(("dS", s.get(("xnspill", i - 1))))
                        d("scalar", lambda g=g: nc.scalar.activation(
                            xn_fm[:, g * 512:(g + 1) * 512], ps_tr[g % 2][:], AF.Copy),
                            inc="sS", waits=tuple(w))
                        s.mark(("tre", gi), "sS")
                    s.mark(("trdone", i), "tS")
                    s.mark(("evdone", i), "sS")

                    d("sync", lambda i=i: nc.sync.dma_start(
                        out=xn_sp[i], in_=xn_fm[:]),
                        inc="dS", inc_n=16, waits=(("sS", s.get(("evdone", i))),))
                    s.mark(("xnspill", i), "dS")

                    w = [("sS", s.get(("evdone", i))), ("dG", s.get("wkv"))]
                    if i >= 1:
                        w.append(("vS", s.get(("pgenfree", i - 1))))
                    for half in range(2):
                        for c in range(C):
                            d("tensor", lambda half=half, c=c: nc.tensor.matmul(
                                ps_gen[:, half * 512:(half + 1) * 512],
                                xn_fm[:, c * 128:(c + 1) * 128],
                                wk_sb[:, c * 1024 + half * 512: c * 1024 + (half + 1) * 512],
                                start=(c == 0), stop=(c == C - 1)),
                                inc=("tS" if (c == C - 1 and half == 1) else None),
                                waits=tuple(w) if (half == 0 and c == 0) else ())
                    s.mark(("kmm", i), "tS")
                    w = [("tS", s.get(("kmm", i)))]
                    if i >= 1:
                        w.append(("tS", s.get(("kvmm", i - 1))))
                    else:
                        w.append(("dS", s.get("a1c")))
                    d("vector", lambda: nc.vector.tensor_add(
                        scr0[:, 0:1024], ps_gen[:], bk_rep[:]),
                        inc="vS", waits=tuple(w))
                    s.mark(("kfree", i), "vS")
                    d("vector", lambda: nc.vector.tensor_scalar(
                        scr1[:], scr0[:, 0:1024], 0.0, None, op0=ALU.min), inc="vS")
                    s.mark(("kxm", i), "vS")
                    d("scalar", lambda: nc.scalar.activation(
                        scr1[:], scr1[:], AF.Exp),
                        inc="sS", waits=(("vS", s.get(("kxm", i))),))
                    s.mark(("kexp", i), "sS")
                    d("vector", lambda: nc.vector.scalar_tensor_tensor(
                        k_bf[:], scr0[:, 0:1024], 1.0, scr1[:],
                        op0=ALU.add, op1=ALU.max),
                        inc="vS", waits=(("sS", s.get(("kexp", i))),))
                    s.mark(("kbf", i), "vS")

                    for half in range(2):
                        for c in range(C):
                            d("tensor", lambda half=half, c=c: nc.tensor.matmul(
                                ps_gen[:, half * 512:(half + 1) * 512],
                                xn_fm[:, c * 128:(c + 1) * 128],
                                wv_sb[:, c * 1024 + half * 512: c * 1024 + (half + 1) * 512],
                                start=(c == 0), stop=(c == C - 1)),
                                inc=("tS" if (c == C - 1 and half == 1) else None),
                                waits=(("vS", s.get(("kfree", i))),)
                                if (half == 0 and c == 0) else ())
                    s.mark(("vmm", i), "tS")
                    w = [("tS", s.get(("vmm", i)))]
                    if i >= 1:
                        w.append(("tS", s.get(("kvmm", i - 1))))
                    d("vector", lambda: nc.vector.tensor_add(
                        v_bf[:], ps_gen[:], bv_rep[:]),
                        inc="vS", waits=tuple(w))
                    s.mark(("pgenfree", i), "vS")
                    s.mark(("vbf", i), "vS")

                    w = [("vS", s.get(("kbf", i))), ("vS", s.get(("vbf", i)))]
                    if i >= 1:
                        w.append(("vS", s.get(("kvacc", i - 1))))
                    for h in range(KH):
                        d("tensor", lambda h=h: nc.tensor.matmul(
                            ps_kv[:, h * 128:(h + 1) * 128],
                            k_bf[:, h * 128:(h + 1) * 128],
                            v_bf[:, h * 128:(h + 1) * 128],
                            start=(h % 4 == 0), stop=(h % 4 == 3)),
                            waits=tuple(w) if h == 0 else ())
                    for h in range(KH):
                        d("tensor", lambda h=h: nc.tensor.matmul(
                            ps_ks[:, h:h + 1],
                            k_bf[:, h * 128:(h + 1) * 128], ones_b[:],
                            start=(h == 0), stop=(h == KH - 1)),
                            inc=("tS" if h == KH - 1 else None))
                    s.mark(("kvmm", i), "tS")
                    w = [("tS", s.get(("kvmm", i)))]
                    if i == 0:
                        w.append(("dG", s.get("ccz")))
                    d("vector", lambda: nc.vector.tensor_add(
                        exch[:, 0:512], exch[:, 0:512], ps_kv[:, 0:512]),
                        inc="vS", waits=tuple(w))
                    d("vector", lambda: nc.vector.tensor_add(
                        exch[:, 512:1024], exch[:, 512:1024], ps_kv[:, 512:1024]),
                        inc="vS")
                    d("vector", lambda: nc.vector.tensor_add(
                        exch[:, 1024:1032], exch[:, 1024:1032], ps_ks[:]),
                        inc="vS")
                    s.mark(("kvacc", i), "vS")

                import os as _os2
                if _os2.environ.get("KNOCC"):
                    s.barrier()
                    if _os2.environ.get("KDBLBAR"):
                        s.barrier()
                        s.barrier()
                    return
                d("vector", lambda: nc.vector.reduce_sum(
                    exch[:, 1032:1033], lv_all[:], axis=AX), inc="vS")
                s.mark("exfill", "vS")
                if _os2.environ.get("KNOF2"):
                    s.barrier()
                    return
                for b in range(4):
                    w = [("vS", s.get("exfill")), ("dS", s.get("a1c"))]
                    if b >= 1:
                        w.append(("dG", s.get(("ccw", b - 1))))
                    d("vector", lambda b=b: nc.vector.tensor_scalar(
                        scr0[:], exch[:], bmask_sb[:, b:b + 1], None, op0=ALU.mult),
                        inc="vS", waits=tuple(w))
                    s.mark(("ccm", b), "vS")
                    d("gpsimd", lambda b=b: nc.gpsimd.dma_start(
                        out=cc_in.ap()[b], in_=scr0[:]),
                        inc="dG", inc_n=16, waits=(("vS", s.get(("ccm", b))),))
                    s.mark(("ccw", b), "dG")
                if not _os2.environ.get("KNOAR"):
                    d("gpsimd", lambda: nc.gpsimd.collective_compute(
                        "AllReduce", ALU.add,
                        replica_groups=[list(range(NCORES))],
                        ins=[cc_in.ap().opt()], outs=[cc_out.ap().opt()]),
                        inc="cc", waits=(("dG", s.get(("ccw", 3))),))
                s.barrier()

            if "1" in PH:
                run_phase(phase_a1)

        with contextlib.ExitStack() as a2o:
            sbo = lambda n, shp, dty: a2o.enter_context(nc.sbuf_tensor(n, shp, dty))
            vp_sb = sbo("vp_sb", [128, 8 * 256], BF16)
            wr_sb = sbo("wr_sb", [64, 48], BF16)
            ak_sb = sbo("ak_sb", [64, NSLOT], BF16)
            rk_sb = sbo("rk_sb", [64, NSLOT], BF16)

            with contextlib.ExitStack() as tmp:
                av_sb = tmp.enter_context(nc.sbuf_tensor("av_sb", [128, C * 1024], BF16))
                wvd_sb = tmp.enter_context(nc.sbuf_tensor("wvd_sb", [128, C * 256], BF16))
                ps_vp = tmp.enter_context(nc.psum_tensor("ps_vp", [128, 256], F32))

                def phase_a2a(s):
                    d = s.do
                    import os as _os5
                    if _os5.environ.get("KA2AEMPTY"):
                        s.barrier()
                        return
                    for dst, src in ((wr_sb, wr_h), (ak_sb, ak_h)):
                        d("gpsimd", lambda dst=dst, src=src: nc.gpsimd.dma_start(
                            out=dst[:], in_=src[:]), inc="dG", inc_n=16)
                    d("gpsimd", lambda: nc.gpsimd.dma_start(out=av_sb[:], in_=av_h[:]),
                      inc="dG", inc_n=16)
                    d("gpsimd", lambda: nc.gpsimd.dma_start(out=wvd_sb[:], in_=wvd_h[:]),
                      inc="dG", inc_n=16)
                    s.mark("avw", "dG")
                    for n in range(8):
                        w = [("dG", s.get("avw"))]
                        if n >= 1:
                            w.append(("vS", s.get(("vpe", n - 1))))
                        for hc in range(C):
                            d("tensor", lambda hc=hc, n=n: nc.tensor.matmul(
                                ps_vp[:],
                                av_sb[:, hc * 1024 + n * 128: hc * 1024 + (n + 1) * 128],
                                wvd_sb[:, hc * 256:(hc + 1) * 256],
                                start=(hc == 0), stop=(hc == C - 1)),
                                inc=("tS" if hc == C - 1 else None),
                                waits=tuple(w) if hc == 0 else ())
                        s.mark(("vpm", n), "tS")
                        d("vector", lambda n=n: nc.vector.tensor_copy(
                            vp_sb[:, n * 256:(n + 1) * 256], ps_vp[:]),
                            inc="vS", waits=(("tS", s.get(("vpm", n))),))
                        s.mark(("vpe", n), "vS")
                    import os as _os4
                    if _os4.environ.get("KNORK"):
                        s.mark("rkdone", "vS")
                        s.barrier()
                        return
                    for piece in range(4):
                        w = [("dG", s.get("avw")), ("vS", s.cnt["vS"])]
                        d("tensor", lambda piece=piece: nc.tensor.matmul(
                            ps_vp[0:48, 0:256], wr_sb[:],
                            ak_sb[:, piece * 256:(piece + 1) * 256],
                            start=True, stop=True),
                            inc="tS", waits=tuple(w))
                        s.mark(("rkm", piece), "tS")
                        d("vector", lambda piece=piece: nc.vector.tensor_copy(
                            rk_sb[0:48, piece * 256:(piece + 1) * 256],
                            ps_vp[0:48, 0:256]),
                            inc="vS", waits=(("tS", s.get(("rkm", piece))),))
                    s.mark("rkdone", "vS")
                    s.barrier()

                if "2" in PH or "a" in PH:
                    run_phase(phase_a2a)

            a2 = a2o
            sbl = lambda n, shp, dty: a2.enter_context(nc.sbuf_tensor(n, shp, dty))
            psl = lambda n, shp: a2.enter_context(nc.psum_tensor(n, shp, F32))
            wu1_sb = sbl("wu1_sb", [128, C * 1024], BF16)
            wvu_sb = sbl("wvu_sb", [128, 2 * D], BF16)
            wqf_sb = sbl("wqf_sb", [128, C * 64], BF16)
            lr_rep = sbl("lr_rep", [128, NSLOT], F32)
            bu1_sb = sbl("bu1_sb", [128, 8], F32)
            ident_f2 = sbl("ident_f2", [128, 128], F32)
            wu2_sb = sbl("wu2_sb", [128, 8], BF16)
            xn_st = sbl("xn_st", [128, 4 * D], BF16)
            zS = sbl("zS", [128, D], F32)
            t2S = sbl("t2S", [128, D], F32)
            gS = sbl("gS", [128, D], BF16)
            sc_tm = sbl("sc_tm", [128, NSLOT], F32)
            zz = sbl("zz", [128, NSLOT], F32)
            pen = sbl("pen", [128, NSLOT], F32)
            w_t = sbl("w_t", [128, NSLOT], F32)
            qf_b = sbl("qf_b", [64, 128], BF16)
            wT_b = sbl("wT_b", [128, NSLOT], BF16)
            y1_b = sbl("y1_b", [128, 256], BF16)
            aux_t = sbl("aux_t", [128, D], F32)
            t8 = sbl("t8", [128, 8], F32)
            ps_u1 = psl("ps_u1", [128, 512])
            ps_sc = psl("ps_sc", [128, 1024])
            ps_tr2 = [psl("ps_tr2a", [128, 128]), psl("ps_tr2b", [128, 128])]
            ps_y1 = psl("ps_y1", [128, 256])
            ps_ax = psl("ps_ax", [128, 512])
            ps_l = psl("ps_l", [128, 1])

            def phase_a2(s):
                d = s.do
                for dst, src in ((wu1_sb, wu1_h), (wvu_sb, wvu_h), (wqf_sb, wqf_h),
                                 (wu2_sb, wu2_h)):
                    d("gpsimd", lambda dst=dst, src=src: nc.gpsimd.dma_start(
                        out=dst[:], in_=src[:]), inc="dG", inc_n=16)
                d("sync", lambda: nc.sync.dma_start(out=ident_f2[:], in_=ident_h[:]),
                  inc="dS", inc_n=16)
                s.mark("a2w", "dG")
                d("sync", lambda: nc.sync.dma_start(out=lr_rep[:], in_=lrrep_h[:]),
                  inc="dS", inc_n=16)
                d("sync", lambda: nc.sync.dma_start(out=bu1_sb[:], in_=bu1r_h[:]),
                  inc="dS", inc_n=16)
                s.mark("a2c", "dS")

                for st in range(NST):
                    for jj in range(4):
                        w = []
                        if st >= 1 and jj == 0:
                            w.append(("tS", s.get(("stdone", st - 1))))
                        d("sync", lambda st=st, jj=jj: nc.sync.dma_start(
                            out=xn_st[:, jj * D:(jj + 1) * D], in_=xn_sp[st * 4 + jj]),
                            inc="dS", inc_n=16, waits=tuple(w))
                    s.mark(("xnld", st), "dS")
                    xn3 = xn_st[:].rearrange("p (j x) -> p j x", j=4)

                    for u in range(8):
                        w = [("dS", s.get(("xnld", st))), ("dG", s.get("a2w")),
                             ("dS", s.get("a2c"))]
                        if u >= 1:
                            w.append(("vS", s.get(("zev", st, u - 1))))
                        elif st >= 1:
                            w.append(("vS", s.get(("zev", st - 1, 7))))
                        for c in range(C):
                            d("tensor", lambda u=u, c=c: nc.tensor.matmul(
                                ps_u1[:],
                                wu1_sb[:, c * 1024 + u * 128: c * 1024 + (u + 1) * 128],
                                xn3[:, :, c * 128:(c + 1) * 128],
                                start=(c == 0), stop=(c == C - 1)),
                                inc=("tS" if c == C - 1 else None),
                                waits=tuple(w) if c == 0 else ())
                        s.mark(("u1m", st, u), "tS")
                        d("vector", lambda u=u: nc.vector.tensor_scalar(
                            zS[:, u * 512:(u + 1) * 512], ps_u1[:],
                            bu1_sb[:, u:u + 1], None, op0=ALU.add),
                            inc="vS", waits=(("tS", s.get(("u1m", st, u))),))
                        s.mark(("zev", st, u), "vS")
                    d("vector", lambda: nc.vector.tensor_mul(t2S[:], zS[:], zS[:]), inc="vS")
                    d("vector", lambda: nc.vector.tensor_scalar(
                        t2S[:], t2S[:], GC1, None, op0=ALU.mult), inc="vS")
                    d("vector", lambda: nc.vector.tensor_scalar(
                        t2S[:], t2S[:], GC0, None, op0=ALU.add), inc="vS")
                    d("vector", lambda: nc.vector.tensor_mul(t2S[:], t2S[:], zS[:]), inc="vS")
                    s.mark(("gel1", st), "vS")
                    d("scalar", lambda: nc.scalar.activation(t2S[:], t2S[:], AF.Tanh),
                      inc="sS", waits=(("vS", s.get(("gel1", st))),))
                    s.mark(("tanh", st), "sS")
                    d("vector", lambda: nc.vector.tensor_scalar(
                        zS[:], zS[:], 0.5, None, op0=ALU.mult), inc="vS")
                    d("vector", lambda: nc.vector.scalar_tensor_tensor(
                        gS[:], t2S[:], 1.0, zS[:], op0=ALU.add, op1=ALU.mult),
                        inc="vS", waits=(("sS", s.get(("tanh", st))),))
                    s.mark(("gbf", st), "vS")
                    for jj in range(4):
                        w = [("vS", s.get(("gbf", st)))]
                        if jj >= 1:
                            w.append(("sS", s.get(("sig", st, jj - 1))))
                        for u in range(8):
                            d("tensor", lambda u=u, jj=jj: nc.tensor.matmul(
                                ps_l[:],
                                gS[:, u * 512 + jj * 128: u * 512 + (jj + 1) * 128],
                                wu2_sb[:, u:u + 1],
                                start=(u == 0), stop=(u == 7)),
                                inc=("tS" if u == 7 else None),
                                waits=tuple(w) if u == 0 else ())
                        s.mark(("lm", st, jj), "tS")
                        d("vector", lambda: nc.vector.tensor_scalar(
                            small[:, 15:16], ps_l[:], float(bu2), None, op0=ALU.add),
                            inc="vS", waits=(("tS", s.get(("lm", st, jj))),))
                        s.mark(("lpre", st, jj), "vS")
                        d("scalar", lambda st=st, jj=jj: nc.scalar.activation(
                            sig_all[:, st * 4 + jj:st * 4 + jj + 1], small[:, 15:16],
                            AF.Sigmoid),
                            inc="sS", waits=(("vS", s.get(("lpre", st, jj))),))
                        s.mark(("sig", st, jj), "sS")

                    import os as _os6
                    if _os6.environ.get("KNOROUTER"):
                        s.mark(("stdone", st), "tS")
                        continue
                    for jj in range(4):
                        i = st * 4 + jj
                        w = [("dS", s.get(("xnld", st))), ("vS", s.get("rkdone")),
                             ("vS", s.get(("zev", st, 7)))]
                        if i >= 1:
                            w.append(("vS", s.get(("qfe", i - 1))))
                        for c in range(C):
                            d("tensor", lambda c=c, jj=jj: nc.tensor.matmul(
                                ps_u1[0:64, 0:128],
                                wqf_sb[:, c * 64:(c + 1) * 64],
                                xn_st[:, jj * D + c * 128: jj * D + (c + 1) * 128],
                                start=(c == 0), stop=(c == C - 1)),
                                inc=("tS" if c == C - 1 else None),
                                waits=tuple(w) if c == 0 else ())
                        s.mark(("qfm", i), "tS")
                        d("vector", lambda: nc.vector.tensor_copy(
                            qf_b[:], ps_u1[0:64, 0:128]),
                            inc="vS", waits=(("tS", s.get(("qfm", i))),))
                        s.mark(("qfe", i), "vS")
                        w = [("vS", s.get(("qfe", i)))]
                        if i >= 1:
                            w.append(("vS", s.get(("y1e", i - 1))))
                        d("tensor", lambda: nc.tensor.matmul(
                            ps_y1[0:48, 0:128], wr_sb[:], qf_b[:],
                            start=True, stop=True), inc="tS", waits=tuple(w))
                        s.mark(("rqm", i), "tS")
                        d("vector", lambda: nc.vector.tensor_copy(
                            wT_b[0:48, 0:128], ps_y1[0:48, 0:128]),
                            inc="vS", waits=(("tS", s.get(("rqm", i))),))
                        s.mark(("rqe", i), "vS")
                        for half in range(2):
                            w = [("vS", s.get(("rqe", i)))]
                            if i >= 1:
                                w.append(("vS", s.get(("zze", i - 1))))
                            d("tensor", lambda half=half: nc.tensor.matmul(
                                ps_sc[:, half * 512:(half + 1) * 512],
                                wT_b[0:48, 0:128],
                                rk_sb[0:48, half * 512:(half + 1) * 512],
                                start=True, stop=True),
                                inc=("tS" if half == 1 else None),
                                waits=tuple(w) if half == 0 else ())
                        s.mark(("scm", i), "tS")
                        for half in range(2):
                            d("vector", lambda half=half: nc.vector.scalar_tensor_tensor(
                                sc_tm[:, half * 512:(half + 1) * 512],
                                ps_sc[:, half * 512:(half + 1) * 512],
                                ISQ_R, lr_rep[:, half * 512:(half + 1) * 512],
                                op0=ALU.mult, op1=ALU.add),
                                inc="vS", waits=(("tS", s.get(("scm", i))),
                                                 ("dS", s.get("a2c"))))
                        s.mark(("sce", i), "vS")
                        for half in range(2):
                            d("tensor", lambda half=half: nc.tensor.matmul(
                                ps_sc[:, half * 512:(half + 1) * 512],
                                qf_b[:],
                                ak_sb[:, half * 512:(half + 1) * 512],
                                start=True, stop=True),
                                inc=("tS" if half == 1 else None),
                                waits=(("vS", s.get(("sce", i))),) if half == 0 else ())
                        s.mark(("a2m", i), "tS")
                        d("vector", lambda: nc.vector.max(out=t8[:], in_=sc_tm[:]),
                          inc="vS")
                        d("vector", lambda: nc.vector.tensor_scalar(
                            pen[:], sc_tm[:], t8[:, 7:8], None, op0=ALU.is_lt), inc="vS")
                        for half in range(2):
                            d("vector", lambda half=half: nc.vector.scalar_tensor_tensor(
                                zz[:, half * 512:(half + 1) * 512],
                                ps_sc[:, half * 512:(half + 1) * 512],
                                ISQ_B, lr_rep[:, half * 512:(half + 1) * 512],
                                op0=ALU.mult, op1=ALU.add),
                                inc="vS", waits=(("tS", s.get(("a2m", i))),))
                        s.mark(("zze", i), "vS")
                        d("vector", lambda: nc.vector.scalar_tensor_tensor(
                            zz[:], pen[:], -BIG, zz[:], op0=ALU.mult, op1=ALU.add),
                            inc="vS")
                        d("vector", lambda: nc.vector.reduce_max(
                            small[:, 9:10], zz[:], axis=AX, negate=True), inc="vS")
                        s.mark(("zmax", i), "vS")
                        d("scalar", lambda: nc.scalar.activation(
                            sc_tm[:], zz[:], AF.Exp, bias=small[:, 9:10]),
                            inc="sS", waits=(("vS", s.get(("zmax", i))),))
                        s.mark(("sexp", i), "sS")
                        d("vector", lambda: nc.vector.reduce_sum(
                            small[:, 10:11], sc_tm[:], axis=AX),
                            inc="vS", waits=(("sS", s.get(("sexp", i))),))
                        d("vector", lambda: nc.vector.reciprocal(
                            small[:, 11:12], small[:, 10:11]), inc="vS")
                        d("vector", lambda: nc.vector.tensor_scalar(
                            w_t[:], sc_tm[:], small[:, 11:12], None, op0=ALU.mult),
                            inc="vS")
                        s.mark(("wt", i), "vS")
                        import os as _os7
                        if _os7.environ.get("KNOWT"):
                            s.mark(("auxsp", i), "dS")
                            continue
                        for n in range(8):
                            ni = i * 8 + n
                            w = [("vS", s.get(("wt", i)))]
                            if ni >= 2:
                                w.append(("vS", s.get(("wte", ni - 2))))
                            d("tensor", lambda n=n: nc.tensor.transpose(
                                ps_tr2[n % 2][:],
                                w_t[:, n * 128:(n + 1) * 128], ident_f2[:]),
                                inc="tS", waits=tuple(w))
                            s.mark(("wtm", ni), "tS")
                            d("vector", lambda n=n: nc.vector.tensor_copy(
                                wT_b[:, n * 128:(n + 1) * 128],
                                ps_tr2[n % 2][:]),
                                inc="vS", waits=(("tS", s.get(("wtm", ni))),))
                            s.mark(("wte", ni), "vS")
                        s.mark(("wTd", i), "vS")
                        for half in range(2):
                            for n in range(8):
                                d("tensor", lambda half=half, n=n: nc.tensor.matmul(
                                    ps_y1[:, half * 128:(half + 1) * 128],
                                    vp_sb[:, n * 256 + half * 128: n * 256 + (half + 1) * 128],
                                    wT_b[:, n * 128:(n + 1) * 128],
                                    start=(n == 0), stop=(n == 7)),
                                    inc=("tS" if n == 7 else None),
                                    waits=(("vS", s.get(("wTd", i))),)
                                    if (half == 0 and n == 0) else ())
                        s.mark(("y1m", i), "tS")
                        d("vector", lambda: nc.vector.tensor_copy(y1_b[:], ps_y1[:]),
                          inc="vS", waits=(("tS", s.get(("y1m", i))),))
                        s.mark(("y1e", i), "vS")
                        for og in range(8):
                            w = [("vS", s.get(("y1e", i)))]
                            if og >= 1:
                                w.append(("vS", s.get(("axe", i, og - 1))))
                            ew = []
                            if og == 0 and i >= 1:
                                ew.append(("dS", s.get(("auxsp", i - 1))))
                            for half in range(2):
                                d("tensor", lambda og=og, half=half: nc.tensor.matmul(
                                    ps_ax[:],
                                    y1_b[:, half * 128:(half + 1) * 128],
                                    wvu_sb[:, half * D + og * 512: half * D + (og + 1) * 512],
                                    start=(half == 0), stop=(half == 1)),
                                    inc=("tS" if half == 1 else None),
                                    waits=tuple(w) if half == 0 else ())
                            s.mark(("axm", i, og), "tS")
                            d("vector", lambda og=og: nc.vector.tensor_copy(
                                aux_t[:, og * 512:(og + 1) * 512], ps_ax[:]),
                                inc="vS",
                                waits=tuple([("tS", s.get(("axm", i, og)))] + ew))
                            s.mark(("axe", i, og), "vS")
                        d("sync", lambda i=i: nc.sync.dma_start(
                            out=aux_sp[i], in_=aux_t[:]),
                            inc="dS", inc_n=16, waits=(("vS", s.get(("axe", i, 7))),))
                        s.mark(("auxsp", i), "dS")
                    s.mark(("stdone", st), "tS")
                s.barrier()

            if "2" in PH and not _os.environ.get("KA2B"):
                run_phase(phase_a2)

        with contextlib.ExitStack() as b1:
            sbl = lambda n, shp, dty: b1.enter_context(nc.sbuf_tensor(n, shp, dty))
            psl = lambda n, shp: b1.enter_context(nc.psum_tensor(n, shp, F32))
            xn_all = sbl("xn_all", [128, NT * D], BF16)
            wqh_sb = [sbl(f"wqh{i}", [128, C * 128], BF16) for i in range(2)]
            bq_sb = sbl("bq_sb", [128, QH], F32)
            exl = sbl("exl", [128, EW], F32)
            exa = sbl("exa", [128, EW], F32)
            qb = sbl("qb", [128, 512], F32)
            qm = sbl("qm", [128, 512], F32)
            qe = sbl("qe", [128, 512], F32)
            qphi = [sbl("qphi0", [128, 512], BF16), sbl("qphi1", [128, 512], BF16)]
            at_b = [sbl(f"at_b{i}", [128, 512], BF16) for i in range(2)]
            ones_sq = sbl("ones_sq", [128, 128], F32)
            ps_q = [psl(f"ps_q{i}", [128, 512]) for i in range(2)]
            ps_at = [psl(f"ps_at{i}", [128, 512]) for i in range(2)]
            ps_m = psl("ps_m", [128, 16])

            def phase_b1(s):
                d = s.do
                d("vector", lambda: nc.vector.memset(ones_sq[:], 1.0), inc="vS")
                d("sync", lambda: nc.sync.dma_start(out=bq_sb[:], in_=bqr_h[:]),
                  inc="dS", inc_n=16)
                s.mark("b1c", "dS")
                import os as _os3
                d("sync", lambda: nc.sync.dma_start(out=exl[:], in_=cc_out.ap()[0]),
                  inc="dS", inc_n=16,
                  waits=(("cc", 1),) if not _os3.environ.get("KNOCC") else ())
                s.mark("ex0", "dS")
                d("vector", lambda: nc.vector.tensor_scalar(
                    exa[:], exl[:], bmask_sb[:, 0:1], None, op0=ALU.mult),
                    inc="vS", waits=(("dS", s.get("ex0")),))
                d("vector", lambda: nc.vector.tensor_copy(
                    small[:, 0:1], exl[:, 1032:1033]), inc="vS")
                s.mark(("exc", 0), "vS")
                for b in range(1, 4):
                    d("sync", lambda b=b: nc.sync.dma_start(
                        out=exl[:], in_=cc_out.ap()[b]),
                        inc="dS", inc_n=16, waits=(("vS", s.get(("exc", b - 1))),))
                    s.mark(("exb", b), "dS")
                    d("vector", lambda b=b: nc.vector.scalar_tensor_tensor(
                        exa[:], exl[:], bmask_sb[:, b:b + 1], exa[:],
                        op0=ALU.mult, op1=ALU.add),
                        inc="vS", waits=(("dS", s.get(("exb", b))),))
                    d("vector", lambda b=b: nc.vector.tensor_copy(
                        small[:, b:b + 1], exl[:, 1032:1033]), inc="vS")
                    s.mark(("exc", b), "vS")
                d("vector", lambda: nc.vector.tensor_copy(kv_bf[:], exa[:, 0:1024]),
                  inc="vS")
                d("vector", lambda: nc.vector.tensor_scalar(
                    exa[:, 1024:1032], exa[:, 1024:1032], 1e-10, None, op0=ALU.add),
                    inc="vS")
                d("vector", lambda: nc.vector.reciprocal(
                    inv_ks[:], exa[:, 1024:1032]), inc="vS")
                s.mark("kvdone", "vS")
                d("vector", lambda: nc.vector.reduce_sum(
                    small[:, 4:5], small[:, 0:4], axis=AX), inc="vS")
                s.mark("lvrow", "vS")
                d("tensor", lambda: nc.tensor.matmul(
                    ps_m[:, 0:1], ones_sq[:], small[:, 4:5],
                    start=True, stop=True),
                    inc="tS", waits=(("vS", s.get("lvrow")),))
                s.mark("lvbc", "tS")
                d("vector", lambda: nc.vector.tensor_scalar(
                    small[:, 6:7], ps_m[:, 0:1], 1.0 / 16384.0, 1e-6,
                    op0=ALU.mult, op1=ALU.add),
                    inc="vS", waits=(("tS", s.get("lvbc")),))
                d("vector", lambda: nc.vector.reciprocal(invmean[:], small[:, 6:7]),
                  inc="vS")
                s.mark("invmean", "vS")

                for j in range(NT):
                    d("sync", lambda j=j: nc.sync.dma_start(
                        out=xn_all[:, j * D:(j + 1) * D], in_=xn_sp[j]),
                        inc="dS", inc_n=16)
                s.mark("xnall", "dS")
                xnr = xn_all[:].rearrange("p (j x) -> p j x", j=NT)

                d("gpsimd", lambda: nc.gpsimd.dma_start(
                    out=wqh_sb[0][:], in_=wq_h[0]), inc="dG", inc_n=16)
                s.mark(("wql", 0), "dG")

                NIDX = QH * NST

                def emit_attn(s, idx):
                    h, st = idx // NST, idx % NST
                    pa = ps_at[idx % 2]
                    w = [("vS", s.get(("qph", idx)))]
                    if idx >= 2:
                        w.append(("vS", s.get(("ate", idx - 2))))
                    d("tensor", lambda h=h, pa=pa, idx=idx: nc.tensor.matmul(
                        pa[:], kv_bf[:, (h // 4) * 128:(h // 4 + 1) * 128],
                        qphi[idx % 2][:], start=True, stop=True),
                        inc="tS", waits=tuple(w))
                    s.mark(("atm", idx), "tS")
                    ab = at_b[idx % 2]
                    w = [("tS", s.get(("atm", idx)))]
                    if idx >= 2:
                        w.append(("dS", s.get(("atsp", idx - 2))))
                    d("vector", lambda pa=pa, h=h, ab=ab: nc.vector.tensor_scalar(
                        ab[:], pa[:], inv_ks[:, h // 4:h // 4 + 1], None,
                        op0=ALU.mult),
                        inc="vS", waits=tuple(w))
                    s.mark(("ate", idx), "vS")
                    d("sync", lambda h=h, st=st, ab=ab: nc.sync.dma_start(
                        out=attn_sp[:, h * T + st * 512: h * T + (st + 1) * 512],
                        in_=ab[:]),
                        inc="dS", inc_n=16, waits=(("vS", s.get(("ate", idx))),))
                    s.mark(("atsp", idx), "dS")

                for h in range(QH):
                    if h + 1 < QH:
                        w = []
                        if h >= 1:
                            w.append(("tS", s.get(("qdone", h - 1))))
                        d("gpsimd", lambda h=h: nc.gpsimd.dma_start(
                            out=wqh_sb[(h + 1) % 2][:], in_=wq_h[h + 1]),
                            inc="dG", inc_n=16, waits=tuple(w))
                        s.mark(("wql", h + 1), "dG")
                    for st in range(NST):
                        idx = h * NST + st
                        pq = ps_q[st % 2]
                        w = [("dG", s.get(("wql", h))), ("dS", s.get("xnall"))]
                        if idx >= 2:
                            w.append(("vS", s.get(("qbe", idx - 2))))
                        for c in range(C):
                            d("tensor", lambda c=c, h=h, st=st, pq=pq: nc.tensor.matmul(
                                pq[:],
                                wqh_sb[h % 2][:, c * 128:(c + 1) * 128],
                                xnr[:, 4 * st:4 * st + 4, c * 128:(c + 1) * 128],
                                start=(c == 0), stop=(c == C - 1)),
                                inc=("tS" if c == C - 1 else None),
                                waits=tuple(w) if c == 0 else ())
                        s.mark(("qmm", idx), "tS")
                        if idx >= 1:
                            emit_attn(s, idx - 1)
                        w = [("tS", s.get(("qmm", idx))), ("dS", s.get("b1c"))]
                        if idx >= 1:
                            w.append(("tS", s.get(("atm", idx - 1))))
                        d("vector", lambda pq=pq, h=h: nc.vector.tensor_scalar(
                            qb[:], pq[:], bq_sb[:, h:h + 1], None, op0=ALU.add),
                            inc="vS", waits=tuple(w))
                        s.mark(("qbe", idx), "vS")
                        d("vector", lambda: nc.vector.tensor_scalar(
                            qm[:], qb[:], 0.0, None, op0=ALU.min), inc="vS")
                        s.mark(("qmi", idx), "vS")
                        d("scalar", lambda: nc.scalar.activation(qe[:], qm[:], AF.Exp),
                          inc="sS", waits=(("vS", s.get(("qmi", idx))),))
                        s.mark(("qex", idx), "sS")
                        w = [("sS", s.get(("qex", idx))), ("vS", s.get("kvdone"))]
                        if idx >= 2:
                            w.append(("tS", s.get(("atm", idx - 2))))
                        d("vector", lambda idx=idx: nc.vector.scalar_tensor_tensor(
                            qphi[idx % 2][:], qb[:], 1.0, qe[:],
                            op0=ALU.add, op1=ALU.max),
                            inc="vS", waits=tuple(w))
                        s.mark(("qph", idx), "vS")
                    s.mark(("qdone", h), "tS")
                emit_attn(s, NIDX - 1)
                s.barrier()

            if "3" in PH:
                run_phase(phase_b1)

        with contextlib.ExitStack() as b2:
            sbl = lambda n, shp, dty: b2.enter_context(nc.sbuf_tensor(n, shp, dty))
            psl = lambda n, shp: b2.enter_context(nc.psum_tensor(n, shp, F32))
            attn_all = sbl("attn_all", [128, QH * T], BF16)
            wo_sb = sbl("wo_sb", [128, C * 512], BF16)
            bo_rep = sbl("bo_rep", [128, D], F32)
            x_sl = [sbl(f"x_sl{i}", [128, 512], F32) for i in range(2)]
            aux_sl = [sbl(f"aux_sl{i}", [128, 512], F32) for i in range(2)]
            xn_sl = sbl("xn_sl", [128, 512], F32)
            s_sb = sbl("s_sb", [128, 512], F32)
            g_sl = sbl("g_sl", [128, 512], F32)
            o_sl = [sbl(f"o_sl{i}", [128, 512], F32) for i in range(2)]
            ps_s = [psl(f"ps_s{i}", [128, 512]) for i in range(2)]

            def phase_b2(s):
                d = s.do
                d("sync", lambda: nc.sync.dma_start(out=bo_rep[:], in_=borep_h[:]),
                  inc="dS", inc_n=16)
                s.mark("b2c", "dS")
                QT4 = QH * T // 4
                for q in range(4):
                    d("sync", lambda q=q: nc.sync.dma_start(
                        out=attn_all[:, q * QT4:(q + 1) * QT4],
                        in_=attn_sp[:, q * QT4:(q + 1) * QT4]),
                        inc="dS", inc_n=16)
                s.mark("atall", "dS")
                d("vector", lambda: nc.vector.tensor_scalar(
                    gate_all[:], lv_all[:], invmean[:, 0:1], None, op0=ALU.mult),
                    inc="vS")
                d("vector", lambda: nc.vector.tensor_scalar(
                    gate_all[:], gate_all[:], 0.5, None, op0=ALU.mult), inc="vS")
                d("vector", lambda: nc.vector.scalar_tensor_tensor(
                    gate_all[:], sig_all[:], 2.5, gate_all[:],
                    op0=ALU.mult, op1=ALU.add), inc="vS")
                d("vector", lambda: nc.vector.tensor_scalar(
                    gate_all[:], gate_all[:], 0.0, None, op0=ALU.max), inc="vS")
                d("vector", lambda: nc.vector.tensor_scalar(
                    gate_all[:], gate_all[:], 5.0, None, op0=ALU.min), inc="vS")
                d("vector", lambda: nc.vector.tensor_scalar(
                    gate_all[:], gate_all[:], 0.5, None, op0=ALU.subtract), inc="vS")
                d("vector", lambda: nc.vector.tensor_scalar(
                    gate_all[:], gate_all[:], 1.0 / 1.5, None, op0=ALU.mult), inc="vS")
                d("vector", lambda: nc.vector.tensor_scalar(
                    gate_all[:], gate_all[:], 0.0, None, op0=ALU.max), inc="vS")
                d("vector", lambda: nc.vector.tensor_scalar(
                    gate_all[:], gate_all[:], 1.0, None, op0=ALU.min), inc="vS")
                d("vector", lambda: nc.vector.tensor_scalar(
                    gate_all[:], gate_all[:], float(gw1), None, op0=ALU.mult), inc="vS")
                d("vector", lambda: nc.vector.tensor_scalar(
                    gate_all[:], gate_all[:], float(gbias), None, op0=ALU.add), inc="vS")
                s.mark("un", "vS")
                d("scalar", lambda: nc.scalar.activation(
                    gate_all[:], gate_all[:], AF.Sigmoid),
                    inc="sS", waits=(("vS", s.get("un")),))
                s.mark("gsig", "sS")
                d("vector", lambda: nc.vector.tensor_scalar(
                    lv_all[:], gate_all[:], 0.05, None, op0=ALU.is_ge),
                    inc="vS", waits=(("sS", s.get("gsig")),))
                d("vector", lambda: nc.vector.tensor_mul(
                    gate_all[:], gate_all[:], lv_all[:]), inc="vS")
                s.mark("gate", "vS")

                for og in range(8):
                    w = []
                    if og >= 1:
                        w.append(("tS", s.get(("ogdone", og - 1))))
                    d("gpsimd", lambda og=og: nc.gpsimd.dma_start(
                        out=wo_sb[:], in_=wo_h[og]),
                        inc="dG", inc_n=16, waits=tuple(w))
                    s.mark(("wol", og), "dG")
                    for t in range(NT):
                        idx = og * NT + t
                        pss = ps_s[t % 2]
                        w = [("dG", s.get(("wol", og))), ("dS", s.get("atall"))]
                        if idx >= 2:
                            w.append(("vS", s.get(("sev", idx - 2))))
                        for a in range(C):
                            d("tensor", lambda a=a, t=t, pss=pss: nc.tensor.matmul(
                                pss[:],
                                attn_all[:, a * T + t * 128: a * T + (t + 1) * 128],
                                wo_sb[:, a * 512:(a + 1) * 512],
                                start=(a == 0), stop=(a == C - 1)),
                                inc=("tS" if a == C - 1 else None),
                                waits=tuple(w) if a == 0 else ())
                        s.mark(("smm", idx), "tS")
                        xb = x_sl[t % 2]
                        ab = aux_sl[t % 2]
                        w = []
                        if idx >= 2:
                            w.append(("vS", s.get(("used", idx - 2))))
                        d("sync", lambda t=t, og=og, xb=xb: nc.sync.dma_start(
                            out=xb[:], in_=x_h[t * 128:(t + 1) * 128,
                                              og * 512:(og + 1) * 512]),
                            inc="dS", inc_n=16, waits=tuple(w))
                        d("sync", lambda t=t, og=og, ab=ab: nc.sync.dma_start(
                            out=ab[:], in_=aux_sp[t][:, og * 512:(og + 1) * 512]),
                            inc="dS", inc_n=16)
                        s.mark(("cld", idx), "dS")
                        d("vector", lambda pss=pss, og=og: nc.vector.tensor_add(
                            s_sb[:], pss[:], bo_rep[:, og * 512:(og + 1) * 512]),
                            inc="vS", waits=(("tS", s.get(("smm", idx))),
                                             ("dS", s.get("b2c"))))
                        s.mark(("sev", idx), "vS")
                        w = [("dS", s.get(("cld", idx))), ("vS", s.get("gate"))]
                        if idx >= 1:
                            w.append(("vS", s.get(("ffn", idx - 1))))
                        d("scalar", lambda t=t, xb=xb: nc.scalar.activation(
                            xn_sl[:], xb[:], AF.Identity,
                            bias=nmr_all[:, t:t + 1], scale=rstd_all[:, t:t + 1]),
                            inc="sS", waits=tuple(w))
                        s.mark(("xnsl", idx), "sS")
                        d("vector", lambda t=t, ab=ab: nc.vector.scalar_tensor_tensor(
                            g_sl[:], ab[:], gate_all[:, t:t + 1], xn_sl[:],
                            op0=ALU.mult, op1=ALU.add),
                            inc="vS", waits=(("sS", s.get(("xnsl", idx))),))
                        s.mark(("ffn", idx), "vS")
                        d("scalar", lambda: nc.scalar.activation(
                            g_sl[:], g_sl[:], AF.Silu),
                            inc="sS", waits=(("vS", s.get(("ffn", idx))),))
                        s.mark(("silu", idx), "sS")
                        ob = o_sl[t % 2]
                        w = [("sS", s.get(("silu", idx)))]
                        if idx >= 2:
                            w.append(("dS", s.get(("odma", idx - 2))))
                        d("vector", lambda ob=ob: nc.vector.tensor_mul(
                            ob[:], g_sl[:], s_sb[:]),
                            inc="vS", waits=tuple(w))
                        d("vector", lambda ob=ob, xb=xb: nc.vector.tensor_add(
                            ob[:], ob[:], xb[:]), inc="vS")
                        s.mark(("used", idx), "vS")
                        d("sync", lambda t=t, og=og, ob=ob: nc.sync.dma_start(
                            out=out_h[t * 128:(t + 1) * 128, og * 512:(og + 1) * 512],
                            in_=ob[:]),
                            inc="dS", inc_n=16, waits=(("vS", s.get(("used", idx))),))
                        s.mark(("odma", idx), "dS")
                    s.mark(("ogdone", og), "tS")
                d("sync", lambda: nc.sync.nop(), waits=(("dS", s.cnt["dS"]),))

            if "4" in PH:
                run_phase(phase_b2)
            else:
                def phase_dummy(s):
                    s.do("vector", lambda: nc.vector.memset(o_sl[0][:], 0.0), inc="vS")
                    for t in range(NT):
                        for og in range(8):
                            s.do("sync", lambda t=t, og=og: nc.sync.dma_start(
                                out=out_h[t * 128:(t + 1) * 128, og * 512:(og + 1) * 512],
                                in_=o_sl[0][:]),
                                inc="dS", inc_n=16,
                                waits=(("vS", s.cnt["vS"]),) if (t == 0 and og == 0) else ())
                    s.do("sync", lambda: nc.sync.dma_start(
                        out=out_h[0:128, 0:EW], in_=cc_out.ap()[0]),
                        inc="dS", inc_n=16, waits=(("dS", s.cnt["dS"]),))
                    s.do("gpsimd", lambda: nc.gpsimd.dma_start(
                        out=out_h[384:512, :], in_=xn_sp[0]), inc="dG", inc_n=16,
                        waits=(("dS", s.cnt["dS"]),))
                    s.do("sync", lambda: nc.sync.dma_start(
                        out=out_h[640:768, :], in_=aux_sp[0]), inc="dS", inc_n=16)
                    for col, tile in ((16, rstd_all), (32, nmr_all), (48, lv_all),
                                      (64, sig_all), (80, gate_all)):
                        s.do("sync", lambda col=col, tile=tile: nc.sync.dma_start(
                            out=out_h[256:384, col:col + NT], in_=tile[:]),
                            inc="dS", inc_n=16)
                    s.do("gpsimd", lambda: nc.gpsimd.dma_start(
                        out=out_h[512:640, 0:2048], in_=attn_sp[:, 0:2048]),
                        inc="dG", inc_n=16)
                    s.do("sync", lambda: nc.sync.nop(), waits=(("dS", s.cnt["dS"]),))
                    s.do("gpsimd", lambda: nc.gpsimd.nop(), waits=(("dG", s.cnt["dG"]),))
                run_phase(phase_dummy)

    return nc


def kernel(**inp):
    x = np.ascontiguousarray(np.asarray(inp["L0_attn_hidden_states"], dtype=np.float32))
    f32 = lambda a: np.ascontiguousarray(np.asarray(a, dtype=np.float32))

    wqT = f32(inp["L0_attn_Wq"]).T
    wkT = f32(inp["L0_attn_Wk"]).T
    wvT = f32(inp["L0_attn_Wv"]).T
    woT = f32(inp["L0_attn_Wo"]).T
    wqfT = f32(inp["L0_ffn_Wq"]).T
    wrT = f32(inp["L0_ffn_Wr"]).T
    akT = np.ascontiguousarray(f32(inp["L0_ffn_aux_keys"]).T)
    avT = f32(inp["L0_ffn_aux_values"]).T
    wvdT = f32(inp["L0_ffn_Wvd"]).T
    wvuT = f32(inp["L0_ffn_Wvu"]).T
    wu1T = f32(inp["L0_ffn_Wu1"]).T
    wu2 = f32(inp["L0_ffn_Wu2"])

    def chunk_pc(wT, width):
        return np.ascontiguousarray(
            wT.reshape(C, 128, width).transpose(1, 0, 2).reshape(128, C * width))

    wk_host = chunk_pc(wkT, 1024)
    wv_host = chunk_pc(wvT, 1024)
    wqf_host = chunk_pc(wqfT, 64)
    av_host = chunk_pc(avT, 1024)
    wvd_host = chunk_pc(wvdT, 256)
    wu1_host = chunk_pc(wu1T, 1024)
    wq_host = np.ascontiguousarray(
        wqT.reshape(C, 128, QH, 128).transpose(2, 1, 0, 3).reshape(QH, 128, C * 128))
    wo_host = np.ascontiguousarray(
        woT.reshape(C, 128, 8, 512).transpose(2, 1, 0, 3).reshape(8, 128, C * 512))
    wvu_host = np.ascontiguousarray(
        wvuT.reshape(2, 128, D).transpose(1, 0, 2).reshape(128, 2 * D))
    wu2_host = np.ascontiguousarray(wu2.reshape(8, 128).T)
    bq_r = np.ascontiguousarray(f32(inp["L0_attn_bq"]).reshape(QH, 128).T)
    bu1_r = np.ascontiguousarray(f32(inp["L0_ffn_bu1"]).reshape(8, 128).T)
    rep = lambda v, n: np.ascontiguousarray(
        np.broadcast_to(f32(v).reshape(1, n), (128, n)))
    bk_rep = rep(inp["L0_attn_bk"], 1024)
    bv_rep = rep(inp["L0_attn_bv"], 1024)
    bo_rep = rep(inp["L0_attn_bo"], D)
    lr_rep = rep(inp["L0_ffn_log_rel"], NSLOT)
    ident = np.eye(128, dtype=np.float32)

    gw1 = float(np.asarray(inp["L0_ffn_gate_w1"]))
    gbias = float(np.asarray(inp["L0_ffn_gate_bias"]))
    bu2 = float(np.asarray(inp["L0_ffn_bu2"]).reshape(-1)[0])

    nc = build_program(gw1, gbias, bu2)

    shared = dict(wk=wk_host, wv=wv_host, wq=wq_host, wo=wo_host,
                  wqf=wqf_host, wr=wrT, ak=akT, av=av_host, wvd=wvd_host,
                  wvu=wvu_host, wu1=wu1_host, wu2=wu2_host, bqr=bq_r,
                  bu1r=bu1_r, bkrep=bk_rep, bvrep=bv_rep, borep=bo_rep,
                  lrrep=lr_rep, ident=ident)
    in_maps = []
    for core in range(NCORES):
        b, half = core // 2, core % 2
        bm = np.zeros((128, 4), dtype=np.float32)
        bm[:, b] = 1.0
        m = dict(shared)
        m["x"] = np.ascontiguousarray(x[b, half * T:(half + 1) * T, :])
        m["bmask"] = bm
        in_maps.append(m)

    res = run_bass_kernel_spmd(nc, in_maps, core_ids=list(range(NCORES)))

    out = np.empty((4, 4096, D), dtype=np.float32)
    for core in range(NCORES):
        b, half = core // 2, core % 2
        out[b, half * T:(half + 1) * T, :] = res.results[core]["out"]
    return out


# version 09
# speedup vs baseline: 1.1124x; 1.0063x over previous
import contextlib
import numpy as np
import concourse.bass as bass
import concourse.mybir as mybir
from concourse.bass_utils import run_bass_kernel_spmd

F32 = mybir.dt.float32
BF16 = mybir.dt.bfloat16
AX = mybir.AxisListType.X
AF = mybir.ActivationFunctionType
ALU = mybir.AluOpType

NCORES = 8
T = 2048
NT = 16
NST = 4
D = 4096
C = 32
KH = 8
QH = 32
NSLOT = 1024
ISQ_R = float(1.0 / np.sqrt(48.0))
ISQ_B = 0.125
BIG = 1.0e30
GC0 = 0.7978845608028654
GC1 = GC0 * 0.044715
EW = 1040

ENGS = ("sync", "gpsimd", "vector", "scalar", "tensor")


class Sched:

    def __init__(self, which, eng, sems, cnt, rec):
        self.which = which
        self.e = eng
        self.sems = sems
        self.cnt = cnt
        self.rec = rec

    def do(self, eng, emit, inc=None, inc_n=1, waits=()):
        if self.which == eng:
            if eng in ("vector", "scalar") and inc is not None and self.cnt[inc] > 0:
                self.e.wait_ge(self.sems[inc], self.cnt[inc])
            for (sname, val) in waits:
                if val is not None and val > 0:
                    self.e.wait_ge(self.sems[sname], val)
            inst = emit()
            if inc is not None:
                inst.then_inc(self.sems[inc], inc_n)
        if inc is not None:
            self.cnt[inc] += inc_n

    def mark(self, key, semname):
        self.rec[key] = self.cnt[semname]

    def get(self, key):
        return self.rec[key]

    def barrier(self):
        base = self.cnt["bY"]
        for j, e in enumerate(ENGS):
            if self.which == e:
                if base + j > 0:
                    self.e.wait_ge(self.sems["bY"], base + j)
                self.e.nop().then_inc(self.sems["bY"], 1)
            self.cnt["bY"] += 1
        if self.which is not None and self.which != ENGS[-1]:
            self.e.wait_ge(self.sems["bY"], base + len(ENGS))


def build_program(gw1, gbias, bu2):
    import os as _os
    PH = _os.environ.get("KPHASES", "1234")
    nc = bass.Bass("TRN2", target_bir_lowering=False)
    dt = nc.dram_tensor

    x_h = dt("x", [T, D], F32, kind="ExternalInput").ap()
    out_h = dt("out", [T, D], F32, kind="ExternalOutput").ap()
    wk_h = dt("wk", [128, C * 1024], F32, kind="ExternalInput").ap()
    wv_h = dt("wv", [128, C * 1024], F32, kind="ExternalInput").ap()
    wq_h = dt("wq", [QH, 128, C * 128], F32, kind="ExternalInput").ap()
    wo_h = dt("wo", [8, 128, C * 512], F32, kind="ExternalInput").ap()
    wqf_h = dt("wqf", [128, C * 64], F32, kind="ExternalInput").ap()
    wr_h = dt("wr", [64, 48], F32, kind="ExternalInput").ap()
    ak_h = dt("ak", [64, NSLOT], F32, kind="ExternalInput").ap()
    av_h = dt("av", [128, C * 1024], F32, kind="ExternalInput").ap()
    wvd_h = dt("wvd", [128, C * 256], F32, kind="ExternalInput").ap()
    wvu_h = dt("wvu", [128, 2 * D], F32, kind="ExternalInput").ap()
    wu1_h = dt("wu1", [128, C * 1024], F32, kind="ExternalInput").ap()
    wu2_h = dt("wu2", [128, 8], F32, kind="ExternalInput").ap()
    bqr_h = dt("bqr", [128, QH], F32, kind="ExternalInput").ap()
    bu1r_h = dt("bu1r", [128, 8], F32, kind="ExternalInput").ap()
    bkrep_h = dt("bkrep", [128, 1024], F32, kind="ExternalInput").ap()
    bvrep_h = dt("bvrep", [128, 1024], F32, kind="ExternalInput").ap()
    borep_h = dt("borep", [128, D], F32, kind="ExternalInput").ap()
    lrrep_h = dt("lrrep", [128, NSLOT], F32, kind="ExternalInput").ap()
    ident_h = dt("ident", [128, 128], F32, kind="ExternalInput").ap()
    bmask_h = dt("bmask", [128, 4], F32, kind="ExternalInput").ap()

    xn_sp = dt("xn_sp", [NT, 128, D], BF16, kind="Internal").ap()
    attn_sp = dt("attn_sp", [128, QH * T], BF16, kind="Internal").ap()
    aux_sp = dt("aux_sp", [NT, 128, D], F32, kind="Internal").ap()
    cc_in = dt("cc_in", [4, 128, EW], F32, kind="Internal")
    cc_out = dt("cc_out", [4, 128, EW], F32, kind="Internal")

    es = contextlib.ExitStack()
    sbt = lambda name, shp, dtype: es.enter_context(nc.sbuf_tensor(name, shp, dtype))

    with es:
        rstd_all = sbt("rstd_all", [128, NT], F32)
        nmr_all = sbt("nmr_all", [128, NT], F32)
        lv_all = sbt("lv_all", [128, NT], F32)
        sig_all = sbt("sig_all", [128, NT], F32)
        gate_all = sbt("gate_all", [128, NT], F32)
        kv_bf = sbt("kv_bf", [128, KH * 128], BF16)
        inv_ks = sbt("inv_ks", [128, KH], F32)
        invmean = sbt("invmean", [128, 1], F32)
        ones_f = sbt("ones_f", [128, 1], F32)
        ones_b = sbt("ones_b", [128, 1], BF16)
        small = sbt("small", [128, 16], F32)
        bmask_sb = sbt("bmask_sb", [128, 4], F32)

        sems = {n: es.enter_context(nc.semaphore(n)) for n in
                ("dS", "dG", "tS", "vS", "sS", "bY", "bG", "cc")}

        cnt_g = {n: 0 for n in sems}
        rec = {}

        def run_phase(body):
            start = dict(cnt_g)
            final = {}

            with nc.Block() as block:
                def emit(which, eng):
                    local = dict(start)
                    body(Sched(which, eng, sems, local, rec))
                    final.update(local)

                @block.sync
                def _(e):
                    emit("sync", e)

                @block.gpsimd
                def _(e):
                    emit("gpsimd", e)

                @block.vector
                def _(e):
                    emit("vector", e)

                @block.scalar
                def _(e):
                    emit("scalar", e)

                @block.tensor
                def _(e):
                    emit("tensor", e)

            cnt_g.clear()
            cnt_g.update(final)

        with contextlib.ExitStack() as a1:
            sbl = lambda n, shp, dty: a1.enter_context(nc.sbuf_tensor(n, shp, dty))
            psl = lambda n, shp: a1.enter_context(nc.psum_tensor(n, shp, F32))
            wk_sb = sbl("wk_sb", [128, C * 1024], BF16)
            wv_sb = sbl("wv_sb", [128, C * 1024], BF16)
            x_t = [sbl("x_t0", [128, D], F32)]
            xn_t = sbl("xn_t", [128, D], F32)
            xn_fm = sbl("xn_fm", [128, D], BF16)
            k_bf = sbl("k_bf", [128, 1024], BF16)
            v_bf = sbl("v_bf", [128, 1024], BF16)
            bk_rep = sbl("bk_rep", [128, 1024], F32)
            bv_rep = sbl("bv_rep", [128, 1024], F32)
            ident_f = sbl("ident_f", [128, 128], F32)
            scr0 = sbl("scr0", [128, EW], F32)
            scr1 = sbl("scr1", [128, 1024], F32)
            exch = sbl("exch", [128, EW], F32)
            ps_tr = [psl(f"ps_tr{i}", [128, 512]) for i in range(2)]
            ps_gen = psl("ps_gen", [128, 1024])
            ps_kv = psl("ps_kv", [128, 1024])
            ps_ks = psl("ps_ks", [128, 8])

            def phase_a1(s):
                d = s.do
                d("gpsimd", lambda: nc.gpsimd.dma_start(out=wk_sb[:], in_=wk_h[:]), inc="dG", inc_n=16)
                d("gpsimd", lambda: nc.gpsimd.dma_start(out=wv_sb[:], in_=wv_h[:]), inc="dG", inc_n=16)
                s.mark("wkv", "dG")
                d("sync", lambda: nc.sync.dma_start(out=bk_rep[:], in_=bkrep_h[:]), inc="dS", inc_n=16)
                d("sync", lambda: nc.sync.dma_start(out=bv_rep[:], in_=bvrep_h[:]), inc="dS", inc_n=16)
                d("sync", lambda: nc.sync.dma_start(out=ident_f[:], in_=ident_h[:]), inc="dS", inc_n=16)
                d("sync", lambda: nc.sync.dma_start(out=bmask_sb[:], in_=bmask_h[:]), inc="dS", inc_n=16)
                s.mark("a1c", "dS")
                d("vector", lambda: nc.vector.memset(ones_f[:], 1.0), inc="vS")
                d("vector", lambda: nc.vector.memset(ones_b[:], 1.0), inc="vS")
                d("vector", lambda: nc.vector.memset(exch[:], 0.0), inc="vS")
                s.mark("exch0", "vS")
                for b in range(4):
                    d("gpsimd", lambda b=b: nc.gpsimd.dma_start(
                        out=cc_in.ap()[b], in_=exch[:]),
                        inc="dG", inc_n=16, waits=(("vS", s.get("exch0")),))
                s.mark("ccz", "dG")

                for i in range(NT):
                    xb = x_t[0]
                    sqc = 13 + (i % 2)
                    w = []
                    if i >= 1:
                        w += [("sS", s.get(("xnact", i - 1))),
                              ("vS", s.get(("xsum", i - 1)))]
                    d("sync", lambda i=i, xb=xb: nc.sync.dma_start(
                        out=xb[:], in_=x_h[i * 128:(i + 1) * 128, :]),
                        inc="dS", inc_n=16, waits=tuple(w))
                    s.mark(("xld", i), "dS")

                    d("vector", lambda xb=xb: nc.vector.reduce_sum(
                        small[:, 0:1], xb[:], axis=AX),
                        inc="vS", waits=(("dS", s.get(("xld", i))),))
                    s.mark(("xsum", i), "vS")
                    w = [("dS", s.get(("xld", i)))]
                    if i >= 1:
                        w.append(("tS", s.get(("trdone", i - 1))))
                    if i >= 2:
                        w.append(("vS", s.get(("varred", i - 2))))
                    d("scalar", lambda xb=xb, sqc=sqc: nc.scalar.activation(
                        xn_t[:], xb[:], AF.Square, accum_out=small[:, sqc:sqc + 1]),
                        inc="sS", waits=tuple(w))
                    s.mark(("ssq", i), "sS")
                    d("vector", lambda: nc.vector.tensor_scalar(
                        small[:, 2:3], small[:, 0:1], 1.0 / D, None, op0=ALU.mult), inc="vS")
                    d("vector", lambda: nc.vector.tensor_mul(
                        small[:, 3:4], small[:, 2:3], small[:, 2:3]), inc="vS")
                    d("vector", lambda sqc=sqc: nc.vector.scalar_tensor_tensor(
                        small[:, 4:5], small[:, sqc:sqc + 1], 1.0 / D, small[:, 3:4],
                        op0=ALU.mult, op1=ALU.subtract),
                        inc="vS", waits=(("sS", s.get(("ssq", i))),))
                    s.mark(("varred", i), "vS")
                    d("vector", lambda: nc.vector.tensor_scalar(
                        small[:, 5:6], small[:, 4:5], 1e-5, None, op0=ALU.add), inc="vS")
                    s.mark(("veps", i), "vS")
                    d("scalar", lambda: nc.scalar.activation(
                        small[:, 6:7], small[:, 5:6], AF.Ln),
                        inc="sS", waits=(("vS", s.get(("veps", i))),))
                    d("scalar", lambda i=i: nc.scalar.activation(
                        rstd_all[:, i:i + 1], small[:, 6:7], AF.Exp, scale=-0.5), inc="sS")
                    s.mark(("rstd", i), "sS")
                    d("vector", lambda i=i: nc.vector.scalar_tensor_tensor(
                        nmr_all[:, i:i + 1], small[:, 2:3], -1.0, rstd_all[:, i:i + 1],
                        op0=ALU.mult, op1=ALU.mult),
                        inc="vS", waits=(("sS", s.get(("rstd", i))),))
                    s.mark(("nmr", i), "vS")
                    d("vector", lambda: nc.vector.reciprocal(
                        small[:, 7:8], small[:, 5:6]), inc="vS")
                    d("vector", lambda: nc.vector.tensor_mul(
                        small[:, 8:9], small[:, 4:5], small[:, 7:8]), inc="vS")
                    s.mark(("ratio", i), "vS")
                    d("scalar", lambda i=i: nc.scalar.activation(
                        lv_all[:, i:i + 1], small[:, 8:9], AF.Ln, bias=1.0),
                        inc="sS", waits=(("vS", s.get(("ratio", i))),))
                    d("scalar", lambda i=i, xb=xb: nc.scalar.activation(
                        xn_t[:], xb[:], AF.Identity,
                        bias=nmr_all[:, i:i + 1], scale=rstd_all[:, i:i + 1]),
                        inc="sS", waits=(("vS", s.get(("nmr", i))),))
                    s.mark(("xnact", i), "sS")

                    for g in range(8):
                        gi = i * 8 + g
                        w = [("sS", s.get(("xnact", i)))]
                        if gi >= 2:
                            w.append(("sS", s.get(("tre", gi - 2))))
                        for q in range(4):
                            cq = 4 * g + q
                            d("tensor", lambda g=g, q=q, cq=cq: nc.tensor.transpose(
                                ps_tr[g % 2][:, q * 128:(q + 1) * 128],
                                xn_t[:, cq * 128:(cq + 1) * 128], ident_f[:]),
                                inc=("tS" if q == 3 else None),
                                waits=tuple(w) if q == 0 else ())
                        s.mark(("trw", gi), "tS")
                        w = [("tS", s.get(("trw", gi)))]
                        if g == 0 and i >= 1:
                            w.append(("dS", s.get(("xnspill", i - 1))))
                        d("scalar", lambda g=g: nc.scalar.activation(
                            xn_fm[:, g * 512:(g + 1) * 512], ps_tr[g % 2][:], AF.Copy),
                            inc="sS", waits=tuple(w))
                        s.mark(("tre", gi), "sS")
                    s.mark(("trdone", i), "tS")
                    s.mark(("evdone", i), "sS")

                    d("sync", lambda i=i: nc.sync.dma_start(
                        out=xn_sp[i], in_=xn_fm[:]),
                        inc="dS", inc_n=16, waits=(("sS", s.get(("evdone", i))),))
                    s.mark(("xnspill", i), "dS")

                    w = [("sS", s.get(("evdone", i))), ("dG", s.get("wkv"))]
                    if i >= 1:
                        w.append(("vS", s.get(("pgenfree", i - 1))))
                    for half in range(2):
                        for c in range(C):
                            d("tensor", lambda half=half, c=c: nc.tensor.matmul(
                                ps_gen[:, half * 512:(half + 1) * 512],
                                xn_fm[:, c * 128:(c + 1) * 128],
                                wk_sb[:, c * 1024 + half * 512: c * 1024 + (half + 1) * 512],
                                start=(c == 0), stop=(c == C - 1)),
                                inc=("tS" if (c == C - 1 and half == 1) else None),
                                waits=tuple(w) if (half == 0 and c == 0) else ())
                    s.mark(("kmm", i), "tS")
                    w = [("tS", s.get(("kmm", i)))]
                    if i >= 1:
                        w.append(("tS", s.get(("kvmm", i - 1))))
                    else:
                        w.append(("dS", s.get("a1c")))
                    d("vector", lambda: nc.vector.tensor_add(
                        scr0[:, 0:1024], ps_gen[:], bk_rep[:]),
                        inc="vS", waits=tuple(w))
                    s.mark(("kfree", i), "vS")
                    d("vector", lambda: nc.vector.tensor_scalar(
                        scr1[:], scr0[:, 0:1024], 0.0, None, op0=ALU.min), inc="vS")
                    s.mark(("kxm", i), "vS")
                    d("scalar", lambda: nc.scalar.activation(
                        scr1[:], scr1[:], AF.Exp),
                        inc="sS", waits=(("vS", s.get(("kxm", i))),))
                    s.mark(("kexp", i), "sS")
                    d("vector", lambda: nc.vector.scalar_tensor_tensor(
                        k_bf[:], scr0[:, 0:1024], 1.0, scr1[:],
                        op0=ALU.add, op1=ALU.max),
                        inc="vS", waits=(("sS", s.get(("kexp", i))),))
                    s.mark(("kbf", i), "vS")

                    for half in range(2):
                        for c in range(C):
                            d("tensor", lambda half=half, c=c: nc.tensor.matmul(
                                ps_gen[:, half * 512:(half + 1) * 512],
                                xn_fm[:, c * 128:(c + 1) * 128],
                                wv_sb[:, c * 1024 + half * 512: c * 1024 + (half + 1) * 512],
                                start=(c == 0), stop=(c == C - 1)),
                                inc=("tS" if (c == C - 1 and half == 1) else None),
                                waits=(("vS", s.get(("kfree", i))),)
                                if (half == 0 and c == 0) else ())
                    s.mark(("vmm", i), "tS")
                    w = [("tS", s.get(("vmm", i)))]
                    if i >= 1:
                        w.append(("tS", s.get(("kvmm", i - 1))))
                    d("vector", lambda: nc.vector.tensor_add(
                        v_bf[:], ps_gen[:], bv_rep[:]),
                        inc="vS", waits=tuple(w))
                    s.mark(("pgenfree", i), "vS")
                    s.mark(("vbf", i), "vS")

                    w = [("vS", s.get(("kbf", i))), ("vS", s.get(("vbf", i)))]
                    if i >= 1:
                        w.append(("vS", s.get(("kvacc", i - 1))))
                    for h in range(KH):
                        d("tensor", lambda h=h: nc.tensor.matmul(
                            ps_kv[:, h * 128:(h + 1) * 128],
                            k_bf[:, h * 128:(h + 1) * 128],
                            v_bf[:, h * 128:(h + 1) * 128],
                            start=(h % 4 == 0), stop=(h % 4 == 3)),
                            waits=tuple(w) if h == 0 else ())
                    for h in range(KH):
                        d("tensor", lambda h=h: nc.tensor.matmul(
                            ps_ks[:, h:h + 1],
                            k_bf[:, h * 128:(h + 1) * 128], ones_b[:],
                            start=(h == 0), stop=(h == KH - 1)),
                            inc=("tS" if h == KH - 1 else None))
                    s.mark(("kvmm", i), "tS")
                    w = [("tS", s.get(("kvmm", i)))]
                    if i == 0:
                        w.append(("dG", s.get("ccz")))
                    d("vector", lambda: nc.vector.tensor_add(
                        exch[:, 0:512], exch[:, 0:512], ps_kv[:, 0:512]),
                        inc="vS", waits=tuple(w))
                    d("vector", lambda: nc.vector.tensor_add(
                        exch[:, 512:1024], exch[:, 512:1024], ps_kv[:, 512:1024]),
                        inc="vS")
                    d("vector", lambda: nc.vector.tensor_add(
                        exch[:, 1024:1032], exch[:, 1024:1032], ps_ks[:]),
                        inc="vS")
                    s.mark(("kvacc", i), "vS")

                import os as _os2
                if _os2.environ.get("KNOCC"):
                    s.barrier()
                    if _os2.environ.get("KDBLBAR"):
                        s.barrier()
                        s.barrier()
                    return
                d("vector", lambda: nc.vector.reduce_sum(
                    exch[:, 1032:1033], lv_all[:], axis=AX), inc="vS")
                s.mark("exfill", "vS")
                if _os2.environ.get("KNOF2"):
                    s.barrier()
                    return
                for b in range(4):
                    w = [("vS", s.get("exfill")), ("dS", s.get("a1c"))]
                    if b >= 1:
                        w.append(("dG", s.get(("ccw", b - 1))))
                    d("vector", lambda b=b: nc.vector.tensor_scalar(
                        scr0[:], exch[:], bmask_sb[:, b:b + 1], None, op0=ALU.mult),
                        inc="vS", waits=tuple(w))
                    s.mark(("ccm", b), "vS")
                    d("gpsimd", lambda b=b: nc.gpsimd.dma_start(
                        out=cc_in.ap()[b], in_=scr0[:]),
                        inc="dG", inc_n=16, waits=(("vS", s.get(("ccm", b))),))
                    s.mark(("ccw", b), "dG")
                if not _os2.environ.get("KNOAR"):
                    d("gpsimd", lambda: nc.gpsimd.collective_compute(
                        "AllReduce", ALU.add,
                        replica_groups=[list(range(NCORES))],
                        ins=[cc_in.ap().opt()], outs=[cc_out.ap().opt()]),
                        inc="cc", waits=(("dG", s.get(("ccw", 3))),))
                s.barrier()

            if "1" in PH:
                run_phase(phase_a1)

        with contextlib.ExitStack() as a2o:
            sbo = lambda n, shp, dty: a2o.enter_context(nc.sbuf_tensor(n, shp, dty))
            vp_sb = sbo("vp_sb", [128, 8 * 256], BF16)
            wr_sb = sbo("wr_sb", [64, 48], BF16)
            ak_sb = sbo("ak_sb", [64, NSLOT], BF16)
            rk_sb = sbo("rk_sb", [64, NSLOT], BF16)

            with contextlib.ExitStack() as tmp:
                av_sb = tmp.enter_context(nc.sbuf_tensor("av_sb", [128, C * 1024], BF16))
                wvd_sb = tmp.enter_context(nc.sbuf_tensor("wvd_sb", [128, C * 256], BF16))
                ps_vp = tmp.enter_context(nc.psum_tensor("ps_vp", [128, 256], F32))

                def phase_a2a(s):
                    d = s.do
                    import os as _os5
                    if _os5.environ.get("KA2AEMPTY"):
                        s.barrier()
                        return
                    for dst, src in ((wr_sb, wr_h), (ak_sb, ak_h)):
                        d("gpsimd", lambda dst=dst, src=src: nc.gpsimd.dma_start(
                            out=dst[:], in_=src[:]), inc="dG", inc_n=16)
                    d("gpsimd", lambda: nc.gpsimd.dma_start(out=av_sb[:], in_=av_h[:]),
                      inc="dG", inc_n=16)
                    d("gpsimd", lambda: nc.gpsimd.dma_start(out=wvd_sb[:], in_=wvd_h[:]),
                      inc="dG", inc_n=16)
                    s.mark("avw", "dG")
                    for n in range(8):
                        w = [("dG", s.get("avw"))]
                        if n >= 1:
                            w.append(("vS", s.get(("vpe", n - 1))))
                        for hc in range(C):
                            d("tensor", lambda hc=hc, n=n: nc.tensor.matmul(
                                ps_vp[:],
                                av_sb[:, hc * 1024 + n * 128: hc * 1024 + (n + 1) * 128],
                                wvd_sb[:, hc * 256:(hc + 1) * 256],
                                start=(hc == 0), stop=(hc == C - 1)),
                                inc=("tS" if hc == C - 1 else None),
                                waits=tuple(w) if hc == 0 else ())
                        s.mark(("vpm", n), "tS")
                        d("vector", lambda n=n: nc.vector.tensor_copy(
                            vp_sb[:, n * 256:(n + 1) * 256], ps_vp[:]),
                            inc="vS", waits=(("tS", s.get(("vpm", n))),))
                        s.mark(("vpe", n), "vS")
                    import os as _os4
                    if _os4.environ.get("KNORK"):
                        s.mark("rkdone", "vS")
                        s.barrier()
                        return
                    for piece in range(4):
                        w = [("dG", s.get("avw")), ("vS", s.cnt["vS"])]
                        d("tensor", lambda piece=piece: nc.tensor.matmul(
                            ps_vp[0:48, 0:256], wr_sb[:],
                            ak_sb[:, piece * 256:(piece + 1) * 256],
                            start=True, stop=True),
                            inc="tS", waits=tuple(w))
                        s.mark(("rkm", piece), "tS")
                        d("vector", lambda piece=piece: nc.vector.tensor_copy(
                            rk_sb[0:48, piece * 256:(piece + 1) * 256],
                            ps_vp[0:48, 0:256]),
                            inc="vS", waits=(("tS", s.get(("rkm", piece))),))
                    s.mark("rkdone", "vS")
                    s.barrier()

                if "2" in PH or "a" in PH:
                    run_phase(phase_a2a)

            a2 = a2o
            sbl = lambda n, shp, dty: a2.enter_context(nc.sbuf_tensor(n, shp, dty))
            psl = lambda n, shp: a2.enter_context(nc.psum_tensor(n, shp, F32))
            wu1_sb = sbl("wu1_sb", [128, C * 1024], BF16)
            wvu_sb = sbl("wvu_sb", [128, 2 * D], BF16)
            wqf_sb = sbl("wqf_sb", [128, C * 64], BF16)
            lr_rep = sbl("lr_rep", [128, NSLOT], F32)
            bu1_sb = sbl("bu1_sb", [128, 8], F32)
            ident_f2 = sbl("ident_f2", [128, 128], F32)
            wu2_sb = sbl("wu2_sb", [128, 8], BF16)
            xn_st = sbl("xn_st", [128, 4 * D], BF16)
            zS = sbl("zS", [128, D], F32)
            t2S = sbl("t2S", [128, D], F32)
            gS = sbl("gS", [128, D], BF16)
            sc_tm = sbl("sc_tm", [128, NSLOT], F32)
            zz = sbl("zz", [128, NSLOT], F32)
            w_t = [sbl("w_t0", [128, NSLOT], F32), sbl("w_t1", [128, NSLOT], F32)]
            qf_b = sbl("qf_b", [64, 128], BF16)
            wT_b = sbl("wT_b", [128, NSLOT], BF16)
            y1_b = sbl("y1_b", [128, 256], BF16)
            aux_t = sbl("aux_t", [128, D], F32)
            t8 = sbl("t8", [128, 8], F32)
            ps_u1 = psl("ps_u1", [128, 512])
            ps_sc = psl("ps_sc", [128, 1024])
            ps_tr2 = [psl("ps_tr2a", [128, 128]), psl("ps_tr2b", [128, 128])]
            ps_y1 = psl("ps_y1", [128, 256])
            ps_ax = psl("ps_ax", [128, 512])
            ps_l = psl("ps_l", [128, 128])

            def phase_a2(s):
                d = s.do
                for dst, src in ((wu1_sb, wu1_h), (wvu_sb, wvu_h), (wqf_sb, wqf_h),
                                 (wu2_sb, wu2_h)):
                    d("gpsimd", lambda dst=dst, src=src: nc.gpsimd.dma_start(
                        out=dst[:], in_=src[:]), inc="dG", inc_n=16)
                d("sync", lambda: nc.sync.dma_start(out=ident_f2[:], in_=ident_h[:]),
                  inc="dS", inc_n=16)
                s.mark("a2w", "dG")
                d("sync", lambda: nc.sync.dma_start(out=lr_rep[:], in_=lrrep_h[:]),
                  inc="dS", inc_n=16)
                d("sync", lambda: nc.sync.dma_start(out=bu1_sb[:], in_=bu1r_h[:]),
                  inc="dS", inc_n=16)
                s.mark("a2c", "dS")

                def emit_wty(s, i):
                    for n in range(8):
                        ni = i * 8 + n
                        w = [("vS", s.get(("wt", i)))]
                        if ni >= 2:
                            w.append(("vS", s.get(("wte", ni - 2))))
                        d("tensor", lambda n=n, i=i: nc.tensor.transpose(
                            ps_tr2[n % 2][:],
                            w_t[i % 2][:, n * 128:(n + 1) * 128], ident_f2[:]),
                            inc="tS", waits=tuple(w))
                        s.mark(("wtm", ni), "tS")
                        d("vector", lambda n=n: nc.vector.tensor_copy(
                            wT_b[:, n * 128:(n + 1) * 128],
                            ps_tr2[n % 2][:]),
                            inc="vS", waits=(("tS", s.get(("wtm", ni))),))
                        s.mark(("wte", ni), "vS")
                    s.mark(("wTd", i), "vS")
                    for half in range(2):
                        for n in range(8):
                            d("tensor", lambda half=half, n=n: nc.tensor.matmul(
                                ps_y1[:, half * 128:(half + 1) * 128],
                                vp_sb[:, n * 256 + half * 128: n * 256 + (half + 1) * 128],
                                wT_b[:, n * 128:(n + 1) * 128],
                                start=(n == 0), stop=(n == 7)),
                                inc=("tS" if n == 7 else None),
                                waits=(("vS", s.get(("wTd", i))),)
                                if (half == 0 and n == 0) else ())
                    s.mark(("y1m", i), "tS")
                    d("vector", lambda: nc.vector.tensor_copy(y1_b[:], ps_y1[:]),
                      inc="vS", waits=(("tS", s.get(("y1m", i))),))
                    s.mark(("y1e", i), "vS")
                    for og in range(8):
                        w = [("vS", s.get(("y1e", i)))]
                        if og >= 1:
                            w.append(("vS", s.get(("axe", i, og - 1))))
                        ew = []
                        if og == 0 and i >= 1:
                            ew.append(("dS", s.get(("auxsp", i - 1))))
                        for half in range(2):
                            d("tensor", lambda og=og, half=half: nc.tensor.matmul(
                                ps_ax[:],
                                y1_b[:, half * 128:(half + 1) * 128],
                                wvu_sb[:, half * D + og * 512: half * D + (og + 1) * 512],
                                start=(half == 0), stop=(half == 1)),
                                inc=("tS" if half == 1 else None),
                                waits=tuple(w) if half == 0 else ())
                        s.mark(("axm", i, og), "tS")
                        d("vector", lambda og=og: nc.vector.tensor_copy(
                            aux_t[:, og * 512:(og + 1) * 512], ps_ax[:]),
                            inc="vS",
                            waits=tuple([("tS", s.get(("axm", i, og)))] + ew))
                        s.mark(("axe", i, og), "vS")
                    d("sync", lambda i=i: nc.sync.dma_start(
                        out=aux_sp[i], in_=aux_t[:]),
                        inc="dS", inc_n=16, waits=(("vS", s.get(("axe", i, 7))),))
                    s.mark(("auxsp", i), "dS")

                for st in range(NST):
                    for jj in range(4):
                        w = []
                        if st >= 1 and jj == 0:
                            w.append(("tS", s.get(("stdone", st - 1))))
                        d("sync", lambda st=st, jj=jj: nc.sync.dma_start(
                            out=xn_st[:, jj * D:(jj + 1) * D], in_=xn_sp[st * 4 + jj]),
                            inc="dS", inc_n=16, waits=tuple(w))
                    s.mark(("xnld", st), "dS")
                    xn3 = xn_st[:].rearrange("p (j x) -> p j x", j=4)

                    for u in range(8):
                        w = [("dS", s.get(("xnld", st))), ("dG", s.get("a2w")),
                             ("dS", s.get("a2c"))]
                        if u >= 1:
                            w.append(("vS", s.get(("zev", st, u - 1))))
                        elif st >= 1:
                            w.append(("vS", s.get(("zev", st - 1, 7))))
                        for c in range(C):
                            d("tensor", lambda u=u, c=c: nc.tensor.matmul(
                                ps_u1[:],
                                wu1_sb[:, c * 1024 + u * 128: c * 1024 + (u + 1) * 128],
                                xn3[:, :, c * 128:(c + 1) * 128],
                                start=(c == 0), stop=(c == C - 1)),
                                inc=("tS" if c == C - 1 else None),
                                waits=tuple(w) if c == 0 else ())
                        s.mark(("u1m", st, u), "tS")
                        d("vector", lambda u=u: nc.vector.tensor_scalar(
                            zS[:, u * 512:(u + 1) * 512], ps_u1[:],
                            bu1_sb[:, u:u + 1], None, op0=ALU.add),
                            inc="vS", waits=(("tS", s.get(("u1m", st, u))),))
                        s.mark(("zev", st, u), "vS")
                    d("vector", lambda: nc.vector.tensor_mul(t2S[:], zS[:], zS[:]), inc="vS")
                    d("vector", lambda: nc.vector.tensor_scalar(
                        t2S[:], t2S[:], GC1, None, op0=ALU.mult), inc="vS")
                    d("vector", lambda: nc.vector.tensor_scalar(
                        t2S[:], t2S[:], GC0, None, op0=ALU.add), inc="vS")
                    d("vector", lambda: nc.vector.tensor_mul(t2S[:], t2S[:], zS[:]), inc="vS")
                    s.mark(("gel1", st), "vS")
                    d("scalar", lambda: nc.scalar.activation(t2S[:], t2S[:], AF.Tanh),
                      inc="sS", waits=(("vS", s.get(("gel1", st))),))
                    s.mark(("tanh", st), "sS")
                    d("vector", lambda: nc.vector.tensor_scalar(
                        zS[:], zS[:], 0.5, None, op0=ALU.mult), inc="vS")
                    d("vector", lambda: nc.vector.scalar_tensor_tensor(
                        gS[:], t2S[:], 1.0, zS[:], op0=ALU.add, op1=ALU.mult),
                        inc="vS", waits=(("sS", s.get(("tanh", st))),))
                    s.mark(("gbf", st), "vS")
                    for jj in range(4):
                        w = [("vS", s.get(("gbf", st)))]
                        if jj >= 1:
                            w.append(("sS", s.get(("sig", st, jj - 1))))
                        for u in range(8):
                            d("tensor", lambda u=u, jj=jj: nc.tensor.matmul(
                                ps_l[:, 0:1],
                                gS[:, u * 512 + jj * 128: u * 512 + (jj + 1) * 128],
                                wu2_sb[:, u:u + 1],
                                start=(u == 0), stop=(u == 7)),
                                inc=("tS" if u == 7 else None),
                                waits=tuple(w) if u == 0 else ())
                        s.mark(("lm", st, jj), "tS")
                        d("vector", lambda: nc.vector.tensor_scalar(
                            small[:, 15:16], ps_l[:, 0:1], float(bu2), None, op0=ALU.add),
                            inc="vS", waits=(("tS", s.get(("lm", st, jj))),))
                        s.mark(("lpre", st, jj), "vS")
                        d("scalar", lambda st=st, jj=jj: nc.scalar.activation(
                            sig_all[:, st * 4 + jj:st * 4 + jj + 1], small[:, 15:16],
                            AF.Sigmoid),
                            inc="sS", waits=(("vS", s.get(("lpre", st, jj))),))
                        s.mark(("sig", st, jj), "sS")

                    import os as _os6
                    if _os6.environ.get("KNOROUTER"):
                        s.mark(("stdone", st), "tS")
                        continue
                    for jj in range(4):
                        i = st * 4 + jj
                        w = [("dS", s.get(("xnld", st))), ("vS", s.get("rkdone")),
                             ("vS", s.get(("zev", st, 7)))]
                        if i >= 1:
                            w.append(("vS", s.get(("qfe", i - 1))))
                        for c in range(C):
                            d("tensor", lambda c=c, jj=jj: nc.tensor.matmul(
                                ps_u1[0:64, 0:128],
                                wqf_sb[:, c * 64:(c + 1) * 64],
                                xn_st[:, jj * D + c * 128: jj * D + (c + 1) * 128],
                                start=(c == 0), stop=(c == C - 1)),
                                inc=("tS" if c == C - 1 else None),
                                waits=tuple(w) if c == 0 else ())
                        s.mark(("qfm", i), "tS")
                        d("vector", lambda: nc.vector.tensor_copy(
                            qf_b[:], ps_u1[0:64, 0:128]),
                            inc="vS", waits=(("tS", s.get(("qfm", i))),))
                        s.mark(("qfe", i), "vS")
                        w = [("vS", s.get(("qfe", i))),
                             ("vS", s.get(("lpre", st, 3)))]
                        if i >= 1:
                            w.append(("vS", s.get(("rqe", i - 1))))
                        d("tensor", lambda: nc.tensor.matmul(
                            ps_l[0:48, 0:128], wr_sb[:], qf_b[:],
                            start=True, stop=True), inc="tS", waits=tuple(w))
                        s.mark(("rqm", i), "tS")
                        d("vector", lambda: nc.vector.tensor_copy(
                            wT_b[0:48, 0:128], ps_l[0:48, 0:128]),
                            inc="vS", waits=(("tS", s.get(("rqm", i))),))
                        s.mark(("rqe", i), "vS")
                        for half in range(2):
                            w = [("vS", s.get(("rqe", i)))]
                            if i >= 1:
                                w.append(("vS", s.get(("zze", i - 1))))
                            d("tensor", lambda half=half: nc.tensor.matmul(
                                ps_sc[:, half * 512:(half + 1) * 512],
                                wT_b[0:48, 0:128],
                                rk_sb[0:48, half * 512:(half + 1) * 512],
                                start=True, stop=True),
                                inc=("tS" if half == 1 else None),
                                waits=tuple(w) if half == 0 else ())
                        s.mark(("scm", i), "tS")
                        for half in range(2):
                            d("vector", lambda half=half: nc.vector.scalar_tensor_tensor(
                                sc_tm[:, half * 512:(half + 1) * 512],
                                ps_sc[:, half * 512:(half + 1) * 512],
                                ISQ_R, lr_rep[:, half * 512:(half + 1) * 512],
                                op0=ALU.mult, op1=ALU.add),
                                inc="vS", waits=(("tS", s.get(("scm", i))),
                                                 ("dS", s.get("a2c"))))
                        s.mark(("sce", i), "vS")
                        for half in range(2):
                            d("tensor", lambda half=half: nc.tensor.matmul(
                                ps_sc[:, half * 512:(half + 1) * 512],
                                qf_b[:],
                                ak_sb[:, half * 512:(half + 1) * 512],
                                start=True, stop=True),
                                inc=("tS" if half == 1 else None),
                                waits=(("vS", s.get(("sce", i))),) if half == 0 else ())
                        s.mark(("a2m", i), "tS")
                        d("vector", lambda: nc.vector.max(out=t8[:], in_=sc_tm[:]),
                          inc="vS")
                        d("vector", lambda: nc.vector.tensor_scalar(
                            sc_tm[:], sc_tm[:], t8[:, 7:8], None, op0=ALU.is_lt), inc="vS")
                        for half in range(2):
                            d("vector", lambda half=half: nc.vector.scalar_tensor_tensor(
                                zz[:, half * 512:(half + 1) * 512],
                                ps_sc[:, half * 512:(half + 1) * 512],
                                ISQ_B, lr_rep[:, half * 512:(half + 1) * 512],
                                op0=ALU.mult, op1=ALU.add),
                                inc="vS", waits=(("tS", s.get(("a2m", i))),))
                        s.mark(("zze", i), "vS")
                        d("vector", lambda: nc.vector.scalar_tensor_tensor(
                            zz[:], sc_tm[:], -BIG, zz[:], op0=ALU.mult, op1=ALU.add),
                            inc="vS")
                        d("vector", lambda: nc.vector.reduce_max(
                            small[:, 9:10], zz[:], axis=AX, negate=True), inc="vS")
                        s.mark(("zmax", i), "vS")
                        d("scalar", lambda: nc.scalar.activation(
                            sc_tm[:], zz[:], AF.Exp, bias=small[:, 9:10]),
                            inc="sS", waits=(("vS", s.get(("zmax", i))),))
                        s.mark(("sexp", i), "sS")
                        d("vector", lambda: nc.vector.reduce_sum(
                            small[:, 10:11], sc_tm[:], axis=AX),
                            inc="vS", waits=(("sS", s.get(("sexp", i))),))
                        d("vector", lambda: nc.vector.reciprocal(
                            small[:, 11:12], small[:, 10:11]), inc="vS")
                        d("vector", lambda i=i: nc.vector.tensor_scalar(
                            w_t[i % 2][:], sc_tm[:], small[:, 11:12], None, op0=ALU.mult),
                            inc="vS")
                        s.mark(("wt", i), "vS")
                        import os as _os7
                        if _os7.environ.get("KNOWT"):
                            s.mark(("auxsp", i), "dS")
                            continue
                        if i >= 1:
                            emit_wty(s, i - 1)
                    s.mark(("stdone", st), "tS")
                emit_wty(s, NT - 1)
                s.barrier()

            if "2" in PH and not _os.environ.get("KA2B"):
                run_phase(phase_a2)

        with contextlib.ExitStack() as b1:
            sbl = lambda n, shp, dty: b1.enter_context(nc.sbuf_tensor(n, shp, dty))
            psl = lambda n, shp: b1.enter_context(nc.psum_tensor(n, shp, F32))
            xn_all = sbl("xn_all", [128, NT * D], BF16)
            wqh_sb = [sbl(f"wqh{i}", [128, C * 128], BF16) for i in range(2)]
            bq_sb = sbl("bq_sb", [128, QH], F32)
            exl = sbl("exl", [128, EW], F32)
            exa = sbl("exa", [128, EW], F32)
            qb = sbl("qb", [128, 512], F32)
            qm = sbl("qm", [128, 512], F32)
            qe = sbl("qe", [128, 512], F32)
            qphi = [sbl("qphi0", [128, 512], BF16), sbl("qphi1", [128, 512], BF16)]
            at_b = [sbl(f"at_b{i}", [128, 512], BF16) for i in range(2)]
            ones_sq = sbl("ones_sq", [128, 128], F32)
            ps_q = [psl(f"ps_q{i}", [128, 512]) for i in range(2)]
            ps_at = [psl(f"ps_at{i}", [128, 512]) for i in range(2)]
            ps_m = psl("ps_m", [128, 16])

            def phase_b1(s):
                d = s.do
                d("vector", lambda: nc.vector.memset(ones_sq[:], 1.0), inc="vS")
                d("sync", lambda: nc.sync.dma_start(out=bq_sb[:], in_=bqr_h[:]),
                  inc="dS", inc_n=16)
                s.mark("b1c", "dS")
                import os as _os3
                d("sync", lambda: nc.sync.dma_start(out=exl[:], in_=cc_out.ap()[0]),
                  inc="dS", inc_n=16,
                  waits=(("cc", 1),) if not _os3.environ.get("KNOCC") else ())
                s.mark("ex0", "dS")
                d("vector", lambda: nc.vector.tensor_scalar(
                    exa[:], exl[:], bmask_sb[:, 0:1], None, op0=ALU.mult),
                    inc="vS", waits=(("dS", s.get("ex0")),))
                d("vector", lambda: nc.vector.tensor_copy(
                    small[:, 0:1], exl[:, 1032:1033]), inc="vS")
                s.mark(("exc", 0), "vS")
                for b in range(1, 4):
                    d("sync", lambda b=b: nc.sync.dma_start(
                        out=exl[:], in_=cc_out.ap()[b]),
                        inc="dS", inc_n=16, waits=(("vS", s.get(("exc", b - 1))),))
                    s.mark(("exb", b), "dS")
                    d("vector", lambda b=b: nc.vector.scalar_tensor_tensor(
                        exa[:], exl[:], bmask_sb[:, b:b + 1], exa[:],
                        op0=ALU.mult, op1=ALU.add),
                        inc="vS", waits=(("dS", s.get(("exb", b))),))
                    d("vector", lambda b=b: nc.vector.tensor_copy(
                        small[:, b:b + 1], exl[:, 1032:1033]), inc="vS")
                    s.mark(("exc", b), "vS")
                d("vector", lambda: nc.vector.tensor_copy(kv_bf[:], exa[:, 0:1024]),
                  inc="vS")
                d("vector", lambda: nc.vector.tensor_scalar(
                    exa[:, 1024:1032], exa[:, 1024:1032], 1e-10, None, op0=ALU.add),
                    inc="vS")
                d("vector", lambda: nc.vector.reciprocal(
                    inv_ks[:], exa[:, 1024:1032]), inc="vS")
                s.mark("kvdone", "vS")
                d("vector", lambda: nc.vector.reduce_sum(
                    small[:, 4:5], small[:, 0:4], axis=AX), inc="vS")
                s.mark("lvrow", "vS")
                d("tensor", lambda: nc.tensor.matmul(
                    ps_m[:, 0:1], ones_sq[:], small[:, 4:5],
                    start=True, stop=True),
                    inc="tS", waits=(("vS", s.get("lvrow")),))
                s.mark("lvbc", "tS")
                d("vector", lambda: nc.vector.tensor_scalar(
                    small[:, 6:7], ps_m[:, 0:1], 1.0 / 16384.0, 1e-6,
                    op0=ALU.mult, op1=ALU.add),
                    inc="vS", waits=(("tS", s.get("lvbc")),))
                d("vector", lambda: nc.vector.reciprocal(invmean[:], small[:, 6:7]),
                  inc="vS")
                s.mark("invmean", "vS")

                for j in range(NT):
                    d("sync", lambda j=j: nc.sync.dma_start(
                        out=xn_all[:, j * D:(j + 1) * D], in_=xn_sp[j]),
                        inc="dS", inc_n=16)
                s.mark("xnall", "dS")
                xnr = xn_all[:].rearrange("p (j x) -> p j x", j=NT)

                d("gpsimd", lambda: nc.gpsimd.dma_start(
                    out=wqh_sb[0][:], in_=wq_h[0]), inc="dG", inc_n=16)
                s.mark(("wql", 0), "dG")

                NIDX = QH * NST

                def emit_attn(s, idx):
                    h, st = idx // NST, idx % NST
                    pa = ps_at[idx % 2]
                    w = [("vS", s.get(("qph", idx)))]
                    if idx >= 2:
                        w.append(("vS", s.get(("ate", idx - 2))))
                    d("tensor", lambda h=h, pa=pa, idx=idx: nc.tensor.matmul(
                        pa[:], kv_bf[:, (h // 4) * 128:(h // 4 + 1) * 128],
                        qphi[idx % 2][:], start=True, stop=True),
                        inc="tS", waits=tuple(w))
                    s.mark(("atm", idx), "tS")
                    ab = at_b[idx % 2]
                    w = [("tS", s.get(("atm", idx)))]
                    if idx >= 2:
                        w.append(("dS", s.get(("atsp", idx - 2))))
                    d("vector", lambda pa=pa, h=h, ab=ab: nc.vector.tensor_scalar(
                        ab[:], pa[:], inv_ks[:, h // 4:h // 4 + 1], None,
                        op0=ALU.mult),
                        inc="vS", waits=tuple(w))
                    s.mark(("ate", idx), "vS")
                    d("sync", lambda h=h, st=st, ab=ab: nc.sync.dma_start(
                        out=attn_sp[:, h * T + st * 512: h * T + (st + 1) * 512],
                        in_=ab[:]),
                        inc="dS", inc_n=16, waits=(("vS", s.get(("ate", idx))),))
                    s.mark(("atsp", idx), "dS")

                for h in range(QH):
                    if h + 1 < QH:
                        w = []
                        if h >= 1:
                            w.append(("tS", s.get(("qdone", h - 1))))
                        d("gpsimd", lambda h=h: nc.gpsimd.dma_start(
                            out=wqh_sb[(h + 1) % 2][:], in_=wq_h[h + 1]),
                            inc="dG", inc_n=16, waits=tuple(w))
                        s.mark(("wql", h + 1), "dG")
                    for st in range(NST):
                        idx = h * NST + st
                        pq = ps_q[st % 2]
                        w = [("dG", s.get(("wql", h))), ("dS", s.get("xnall"))]
                        if idx >= 2:
                            w.append(("vS", s.get(("qbe", idx - 2))))
                        for c in range(C):
                            d("tensor", lambda c=c, h=h, st=st, pq=pq: nc.tensor.matmul(
                                pq[:],
                                wqh_sb[h % 2][:, c * 128:(c + 1) * 128],
                                xnr[:, 4 * st:4 * st + 4, c * 128:(c + 1) * 128],
                                start=(c == 0), stop=(c == C - 1)),
                                inc=("tS" if c == C - 1 else None),
                                waits=tuple(w) if c == 0 else ())
                        s.mark(("qmm", idx), "tS")
                        if idx >= 1:
                            emit_attn(s, idx - 1)
                        w = [("tS", s.get(("qmm", idx))), ("dS", s.get("b1c"))]
                        if idx >= 1:
                            w.append(("tS", s.get(("atm", idx - 1))))
                        d("vector", lambda pq=pq, h=h: nc.vector.tensor_scalar(
                            qb[:], pq[:], bq_sb[:, h:h + 1], None, op0=ALU.add),
                            inc="vS", waits=tuple(w))
                        s.mark(("qbe", idx), "vS")
                        d("vector", lambda: nc.vector.tensor_scalar(
                            qm[:], qb[:], 0.0, None, op0=ALU.min), inc="vS")
                        s.mark(("qmi", idx), "vS")
                        d("scalar", lambda: nc.scalar.activation(qe[:], qm[:], AF.Exp),
                          inc="sS", waits=(("vS", s.get(("qmi", idx))),))
                        s.mark(("qex", idx), "sS")
                        w = [("sS", s.get(("qex", idx))), ("vS", s.get("kvdone"))]
                        if idx >= 2:
                            w.append(("tS", s.get(("atm", idx - 2))))
                        d("vector", lambda idx=idx: nc.vector.scalar_tensor_tensor(
                            qphi[idx % 2][:], qb[:], 1.0, qe[:],
                            op0=ALU.add, op1=ALU.max),
                            inc="vS", waits=tuple(w))
                        s.mark(("qph", idx), "vS")
                    s.mark(("qdone", h), "tS")
                emit_attn(s, NIDX - 1)
                s.barrier()

            if "3" in PH:
                run_phase(phase_b1)

        with contextlib.ExitStack() as b2:
            sbl = lambda n, shp, dty: b2.enter_context(nc.sbuf_tensor(n, shp, dty))
            psl = lambda n, shp: b2.enter_context(nc.psum_tensor(n, shp, F32))
            attn_all = sbl("attn_all", [128, QH * T], BF16)
            wo_sb = sbl("wo_sb", [128, C * 512], BF16)
            bo_rep = sbl("bo_rep", [128, D], F32)
            x_sl = [sbl(f"x_sl{i}", [128, 512], F32) for i in range(2)]
            aux_sl = [sbl(f"aux_sl{i}", [128, 512], F32) for i in range(2)]
            xn_sl = sbl("xn_sl", [128, 512], F32)
            s_sb = sbl("s_sb", [128, 512], F32)
            g_sl = sbl("g_sl", [128, 512], F32)
            o_sl = [sbl(f"o_sl{i}", [128, 512], F32) for i in range(2)]
            ps_s = [psl(f"ps_s{i}", [128, 512]) for i in range(2)]

            def phase_b2(s):
                d = s.do
                d("sync", lambda: nc.sync.dma_start(out=bo_rep[:], in_=borep_h[:]),
                  inc="dS", inc_n=16)
                s.mark("b2c", "dS")
                QT4 = QH * T // 4
                for q in range(4):
                    d("sync", lambda q=q: nc.sync.dma_start(
                        out=attn_all[:, q * QT4:(q + 1) * QT4],
                        in_=attn_sp[:, q * QT4:(q + 1) * QT4]),
                        inc="dS", inc_n=16)
                s.mark("atall", "dS")
                d("vector", lambda: nc.vector.tensor_scalar(
                    gate_all[:], lv_all[:], invmean[:, 0:1], None, op0=ALU.mult),
                    inc="vS")
                d("vector", lambda: nc.vector.tensor_scalar(
                    gate_all[:], gate_all[:], 0.5, None, op0=ALU.mult), inc="vS")
                d("vector", lambda: nc.vector.scalar_tensor_tensor(
                    gate_all[:], sig_all[:], 2.5, gate_all[:],
                    op0=ALU.mult, op1=ALU.add), inc="vS")
                d("vector", lambda: nc.vector.tensor_scalar(
                    gate_all[:], gate_all[:], 0.0, None, op0=ALU.max), inc="vS")
                d("vector", lambda: nc.vector.tensor_scalar(
                    gate_all[:], gate_all[:], 5.0, None, op0=ALU.min), inc="vS")
                d("vector", lambda: nc.vector.tensor_scalar(
                    gate_all[:], gate_all[:], 0.5, None, op0=ALU.subtract), inc="vS")
                d("vector", lambda: nc.vector.tensor_scalar(
                    gate_all[:], gate_all[:], 1.0 / 1.5, None, op0=ALU.mult), inc="vS")
                d("vector", lambda: nc.vector.tensor_scalar(
                    gate_all[:], gate_all[:], 0.0, None, op0=ALU.max), inc="vS")
                d("vector", lambda: nc.vector.tensor_scalar(
                    gate_all[:], gate_all[:], 1.0, None, op0=ALU.min), inc="vS")
                d("vector", lambda: nc.vector.tensor_scalar(
                    gate_all[:], gate_all[:], float(gw1), None, op0=ALU.mult), inc="vS")
                d("vector", lambda: nc.vector.tensor_scalar(
                    gate_all[:], gate_all[:], float(gbias), None, op0=ALU.add), inc="vS")
                s.mark("un", "vS")
                d("scalar", lambda: nc.scalar.activation(
                    gate_all[:], gate_all[:], AF.Sigmoid),
                    inc="sS", waits=(("vS", s.get("un")),))
                s.mark("gsig", "sS")
                d("vector", lambda: nc.vector.tensor_scalar(
                    lv_all[:], gate_all[:], 0.05, None, op0=ALU.is_ge),
                    inc="vS", waits=(("sS", s.get("gsig")),))
                d("vector", lambda: nc.vector.tensor_mul(
                    gate_all[:], gate_all[:], lv_all[:]), inc="vS")
                s.mark("gate", "vS")

                for og in range(8):
                    w = []
                    if og >= 1:
                        w.append(("tS", s.get(("ogdone", og - 1))))
                    d("gpsimd", lambda og=og: nc.gpsimd.dma_start(
                        out=wo_sb[:], in_=wo_h[og]),
                        inc="dG", inc_n=16, waits=tuple(w))
                    s.mark(("wol", og), "dG")
                    for t in range(NT):
                        idx = og * NT + t
                        pss = ps_s[t % 2]
                        w = [("dG", s.get(("wol", og))), ("dS", s.get("atall"))]
                        if idx >= 2:
                            w.append(("vS", s.get(("sev", idx - 2))))
                        for a in range(C):
                            d("tensor", lambda a=a, t=t, pss=pss: nc.tensor.matmul(
                                pss[:],
                                attn_all[:, a * T + t * 128: a * T + (t + 1) * 128],
                                wo_sb[:, a * 512:(a + 1) * 512],
                                start=(a == 0), stop=(a == C - 1)),
                                inc=("tS" if a == C - 1 else None),
                                waits=tuple(w) if a == 0 else ())
                        s.mark(("smm", idx), "tS")
                        xb = x_sl[t % 2]
                        ab = aux_sl[t % 2]
                        w = []
                        if idx >= 2:
                            w.append(("vS", s.get(("used", idx - 2))))
                        d("sync", lambda t=t, og=og, xb=xb: nc.sync.dma_start(
                            out=xb[:], in_=x_h[t * 128:(t + 1) * 128,
                                              og * 512:(og + 1) * 512]),
                            inc="dS", inc_n=16, waits=tuple(w))
                        d("sync", lambda t=t, og=og, ab=ab: nc.sync.dma_start(
                            out=ab[:], in_=aux_sp[t][:, og * 512:(og + 1) * 512]),
                            inc="dS", inc_n=16)
                        s.mark(("cld", idx), "dS")
                        d("vector", lambda pss=pss, og=og: nc.vector.tensor_add(
                            s_sb[:], pss[:], bo_rep[:, og * 512:(og + 1) * 512]),
                            inc="vS", waits=(("tS", s.get(("smm", idx))),
                                             ("dS", s.get("b2c"))))
                        s.mark(("sev", idx), "vS")
                        w = [("dS", s.get(("cld", idx))), ("vS", s.get("gate"))]
                        if idx >= 1:
                            w.append(("vS", s.get(("ffn", idx - 1))))
                        d("scalar", lambda t=t, xb=xb: nc.scalar.activation(
                            xn_sl[:], xb[:], AF.Identity,
                            bias=nmr_all[:, t:t + 1], scale=rstd_all[:, t:t + 1]),
                            inc="sS", waits=tuple(w))
                        s.mark(("xnsl", idx), "sS")
                        d("vector", lambda t=t, ab=ab: nc.vector.scalar_tensor_tensor(
                            g_sl[:], ab[:], gate_all[:, t:t + 1], xn_sl[:],
                            op0=ALU.mult, op1=ALU.add),
                            inc="vS", waits=(("sS", s.get(("xnsl", idx))),))
                        s.mark(("ffn", idx), "vS")
                        d("scalar", lambda: nc.scalar.activation(
                            g_sl[:], g_sl[:], AF.Silu),
                            inc="sS", waits=(("vS", s.get(("ffn", idx))),))
                        s.mark(("silu", idx), "sS")
                        ob = o_sl[t % 2]
                        w = [("sS", s.get(("silu", idx)))]
                        if idx >= 2:
                            w.append(("dS", s.get(("odma", idx - 2))))
                        d("vector", lambda ob=ob: nc.vector.tensor_mul(
                            ob[:], g_sl[:], s_sb[:]),
                            inc="vS", waits=tuple(w))
                        d("vector", lambda ob=ob, xb=xb: nc.vector.tensor_add(
                            ob[:], ob[:], xb[:]), inc="vS")
                        s.mark(("used", idx), "vS")
                        d("sync", lambda t=t, og=og, ob=ob: nc.sync.dma_start(
                            out=out_h[t * 128:(t + 1) * 128, og * 512:(og + 1) * 512],
                            in_=ob[:]),
                            inc="dS", inc_n=16, waits=(("vS", s.get(("used", idx))),))
                        s.mark(("odma", idx), "dS")
                    s.mark(("ogdone", og), "tS")
                d("sync", lambda: nc.sync.nop(), waits=(("dS", s.cnt["dS"]),))

            if "4" in PH:
                run_phase(phase_b2)
            else:
                def phase_dummy(s):
                    s.do("vector", lambda: nc.vector.memset(o_sl[0][:], 0.0), inc="vS")
                    for t in range(NT):
                        for og in range(8):
                            s.do("sync", lambda t=t, og=og: nc.sync.dma_start(
                                out=out_h[t * 128:(t + 1) * 128, og * 512:(og + 1) * 512],
                                in_=o_sl[0][:]),
                                inc="dS", inc_n=16,
                                waits=(("vS", s.cnt["vS"]),) if (t == 0 and og == 0) else ())
                    s.do("sync", lambda: nc.sync.dma_start(
                        out=out_h[0:128, 0:EW], in_=cc_out.ap()[0]),
                        inc="dS", inc_n=16, waits=(("dS", s.cnt["dS"]),))
                    s.do("gpsimd", lambda: nc.gpsimd.dma_start(
                        out=out_h[384:512, :], in_=xn_sp[0]), inc="dG", inc_n=16,
                        waits=(("dS", s.cnt["dS"]),))
                    s.do("sync", lambda: nc.sync.dma_start(
                        out=out_h[640:768, :], in_=aux_sp[0]), inc="dS", inc_n=16)
                    for col, tile in ((16, rstd_all), (32, nmr_all), (48, lv_all),
                                      (64, sig_all), (80, gate_all)):
                        s.do("sync", lambda col=col, tile=tile: nc.sync.dma_start(
                            out=out_h[256:384, col:col + NT], in_=tile[:]),
                            inc="dS", inc_n=16)
                    s.do("gpsimd", lambda: nc.gpsimd.dma_start(
                        out=out_h[512:640, 0:2048], in_=attn_sp[:, 0:2048]),
                        inc="dG", inc_n=16)
                    s.do("sync", lambda: nc.sync.nop(), waits=(("dS", s.cnt["dS"]),))
                    s.do("gpsimd", lambda: nc.gpsimd.nop(), waits=(("dG", s.cnt["dG"]),))
                run_phase(phase_dummy)

    return nc


def kernel(**inp):
    x = np.ascontiguousarray(np.asarray(inp["L0_attn_hidden_states"], dtype=np.float32))
    f32 = lambda a: np.ascontiguousarray(np.asarray(a, dtype=np.float32))

    wqT = f32(inp["L0_attn_Wq"]).T
    wkT = f32(inp["L0_attn_Wk"]).T
    wvT = f32(inp["L0_attn_Wv"]).T
    woT = f32(inp["L0_attn_Wo"]).T
    wqfT = f32(inp["L0_ffn_Wq"]).T
    wrT = f32(inp["L0_ffn_Wr"]).T
    akT = np.ascontiguousarray(f32(inp["L0_ffn_aux_keys"]).T)
    avT = f32(inp["L0_ffn_aux_values"]).T
    wvdT = f32(inp["L0_ffn_Wvd"]).T
    wvuT = f32(inp["L0_ffn_Wvu"]).T
    wu1T = f32(inp["L0_ffn_Wu1"]).T
    wu2 = f32(inp["L0_ffn_Wu2"])

    def chunk_pc(wT, width):
        return np.ascontiguousarray(
            wT.reshape(C, 128, width).transpose(1, 0, 2).reshape(128, C * width))

    wk_host = chunk_pc(wkT, 1024)
    wv_host = chunk_pc(wvT, 1024)
    wqf_host = chunk_pc(wqfT, 64)
    av_host = chunk_pc(avT, 1024)
    wvd_host = chunk_pc(wvdT, 256)
    wu1_host = chunk_pc(wu1T, 1024)
    wq_host = np.ascontiguousarray(
        wqT.reshape(C, 128, QH, 128).transpose(2, 1, 0, 3).reshape(QH, 128, C * 128))
    wo_host = np.ascontiguousarray(
        woT.reshape(C, 128, 8, 512).transpose(2, 1, 0, 3).reshape(8, 128, C * 512))
    wvu_host = np.ascontiguousarray(
        wvuT.reshape(2, 128, D).transpose(1, 0, 2).reshape(128, 2 * D))
    wu2_host = np.ascontiguousarray(wu2.reshape(8, 128).T)
    bq_r = np.ascontiguousarray(f32(inp["L0_attn_bq"]).reshape(QH, 128).T)
    bu1_r = np.ascontiguousarray(f32(inp["L0_ffn_bu1"]).reshape(8, 128).T)
    rep = lambda v, n: np.ascontiguousarray(
        np.broadcast_to(f32(v).reshape(1, n), (128, n)))
    bk_rep = rep(inp["L0_attn_bk"], 1024)
    bv_rep = rep(inp["L0_attn_bv"], 1024)
    bo_rep = rep(inp["L0_attn_bo"], D)
    lr_rep = rep(inp["L0_ffn_log_rel"], NSLOT)
    ident = np.eye(128, dtype=np.float32)

    gw1 = float(np.asarray(inp["L0_ffn_gate_w1"]))
    gbias = float(np.asarray(inp["L0_ffn_gate_bias"]))
    bu2 = float(np.asarray(inp["L0_ffn_bu2"]).reshape(-1)[0])

    nc = build_program(gw1, gbias, bu2)

    shared = dict(wk=wk_host, wv=wv_host, wq=wq_host, wo=wo_host,
                  wqf=wqf_host, wr=wrT, ak=akT, av=av_host, wvd=wvd_host,
                  wvu=wvu_host, wu1=wu1_host, wu2=wu2_host, bqr=bq_r,
                  bu1r=bu1_r, bkrep=bk_rep, bvrep=bv_rep, borep=bo_rep,
                  lrrep=lr_rep, ident=ident)
    in_maps = []
    for core in range(NCORES):
        b, half = core // 2, core % 2
        bm = np.zeros((128, 4), dtype=np.float32)
        bm[:, b] = 1.0
        m = dict(shared)
        m["x"] = np.ascontiguousarray(x[b, half * T:(half + 1) * T, :])
        m["bmask"] = bm
        in_maps.append(m)

    res = run_bass_kernel_spmd(nc, in_maps, core_ids=list(range(NCORES)))

    out = np.empty((4, 4096, D), dtype=np.float32)
    for core in range(NCORES):
        b, half = core // 2, core % 2
        out[b, half * T:(half + 1) * T, :] = res.results[core]["out"]
    return out
